# version 10 (final)
# speedup vs baseline: 1.4109x; 1.0007x over previous
import numpy as np
import ml_dtypes
import concourse.bass as bass
import concourse.mybir as mybir
from concourse.bass_utils import run_bass_kernel_spmd
from contextlib import ExitStack

F32 = mybir.dt.float32
BF16 = mybir.dt.bfloat16
AF = mybir.ActivationFunctionType
ALU = mybir.AluOpType
AX = mybir.AxisListType

P = 128
C = 2048
NKC = 16
TLOC = 2048
TT = 512
NTT = 4
CH = 128
NCH = 16
SUB = 32
DH = 64
IFF = 8192
NIC = 64
MAGIC = 12582912.0
LNEPS = 1e-5
QEPS = 1e-5
NCORES = 8

ENGS = ("tensor", "vector", "scalar", "gpsimd", "sync")

CO_ID, CO_UBD, CO_IND4, CO_EXP4, CO_ONES4, CO_TRI4, CO_ONEC, CO_ZERO = 0, 128, 256, 260, 388, 516, 520, 521
NCONST = 649
NCONSTF = 258

V_MAAX, V_MAAR, V_MAAK, V_MAAV, V_MAAW, V_QB, V_KB, V_VB, V_LNG, V_LNB, V_TD = range(11)
MAA_COL = {0: V_MAAR, 1: V_MAAK, 2: V_MAAV, 3: V_MAAW}


class Sem:
    def __init__(self, h):
        self.h = h
        self.n = 0


class KB:

    def __init__(self, nc, es):
        self.nc = nc
        self.es = es
        self.q = {e: [] for e in ENGS}
        self.last = {e: None for e in ENGS}
        self.ms = None
        self.stopped = False

    def mksem(self, name):
        return Sem(self.es.enter_context(self.nc.semaphore(name)))

    def mkqueue(self, name, n=8):
        return {"sems": [self.mksem(name + str(i)) for i in range(n)],
                "i": 0, "n": n}

    def dma(self, eng, f):
        if self.stopped:
            return None
        q = self.dq[eng]
        i = q["i"]
        q["i"] += 1
        j = i % q["n"]
        sem = q["sems"][j]
        if i >= q["n"]:
            self.wait(eng, sem, 16 * (i // q["n"]))
        self.op(eng, f, inc=(sem, 16))
        return (sem, sem.n)

    def wq(self, eng, src_eng):
        if self.stopped:
            return
        for sem in self.dq[src_eng]["sems"]:
            if sem.n > 0:
                self.wait(eng, sem, sem.n)

    def op(self, eng, f, inc=None):
        if self.stopped:
            return None
        rec = {"f": f, "inc": None}
        if eng != "tensor":
            ms = self.ms[eng]
            rec["cw"] = (ms, ms.n)
            if inc is None:
                ms.n += 1
                rec["inc"] = (ms, 1)
        if inc is not None:
            sem, k = inc
            sem.n += k
            rec["inc"] = (sem, k)
        self.q[eng].append(rec)
        self.last[eng] = rec
        return rec

    def wait(self, eng, sem, val=None):
        if self.stopped:
            return
        v = sem.n if val is None else val
        self.q[eng].append({"f": lambda e, s=sem, v=v: e.wait_ge(s.h, v), "inc": None})

    def seal(self, eng):
        if self.stopped or self.last[eng] is None:
            return (self.ms[eng], self.ms[eng].n)
        rec = self.last[eng]
        if rec["inc"] is None:
            ms = self.ms[eng]
            ms.n += 1
            rec["inc"] = (ms, 1)
        sem, _ = rec["inc"]
        return sem, sem.n

    def guard(self, consumer, producer):
        if self.stopped or self.last[producer] is None:
            return
        s, v = self.seal(producer)
        self.wait(consumer, s, v)

    def bar(self):
        if self.stopped:
            return
        marks = {}
        for e in ENGS:
            if self.last[e] is not None:
                marks[e] = self.seal(e)
        for e in ENGS:
            for e2, (s, v) in marks.items():
                if e2 != e:
                    self.wait(e, s, v)
            for e2 in self.dq:
                self.wq(e, e2)

    def replay(self, eng_obj, eng_name):
        for rec in self.q[eng_name]:
            ins = rec["f"](eng_obj)
            cw = rec.get("cw")
            if cw is not None and cw[1] > 0:
                ins._wait_ge(cw[0].h, cw[1])
            if rec["inc"] is not None:
                s, k = rec["inc"]
                ins.then_inc(s.h, k)


def build_program():
    import os
    _PL = os.environ.get("MK_PHASE_LIMIT")
    _PL = float(_PL) if _PL else None
    nc = bass.Bass("TRN2", target_bir_lowering=False, debug=False)
    es = ExitStack()
    T = KB(nc, es)

    def din(name, shape, dt):
        return nc.dram_tensor(name, list(shape), dt, kind="ExternalInput").ap()

    def dint(name, shape, dt, **kw):
        return nc.dram_tensor(name, list(shape), dt, kind="Internal", **kw).ap()

    x_in = din("x", (TLOC, C), F32)
    prevrow = din("prevrow", (1, C), F32)
    vecs = din("vecs", (P, 11 * NKC), F32)
    gates = din("gates", (P, 16), F32)
    consts = din("consts", (P, NCONST), BF16)
    constf = din("constf", (P, NCONSTF), F32)
    masku_in = din("masku", (P, 128), mybir.dt.uint8)
    maarows = din("maarows", (4, C), F32)
    kbrow = din("kbrow", (1, C), F32)
    vbrow = din("vbrow", (1, C), F32)
    tdrow = din("tdrow", (1, C), F32)
    w_maa1 = din("w_maa1", (C, 128), F32)
    w_maa2 = din("w_maa2", (4, 32, C), F32)
    w_td1 = din("w_td1", (C, DH), F32)
    w_td2 = din("w_td2", (DH, C), F32)
    w_q = din("w_q", (C, C), F32)
    w_k = din("w_k", (C, C), F32)
    w_v = din("w_v", (C, C), F32)
    w_o = din("w_o", (C, C), F32)
    w_gT = din("w_gT", (C, IFF), F32)
    w_uT = din("w_uT", (C, IFF), F32)
    w_dT = din("w_dT", (IFF, C), F32)

    out = nc.dram_tensor("out", [TLOC, C], F32, kind="ExternalOutput").ap()

    sXR = dint("sXR", (NKC, P, TLOC), BF16)
    sXK = dint("sXK", (NKC, P, TLOC), BF16)
    sXV = dint("sXV", (NKC, P, TLOC), BF16)
    sW = dint("sW", (TLOC, C), BF16)
    sQ = dint("sQ", (NKC, P, TLOC), BF16)
    sK0 = dint("sK0", (TLOC, C), BF16)
    sV = dint("sV", (TLOC, C), BF16)
    sY = dint("sY", (TLOC, C), BF16)
    sX1 = dint("sX1", (TLOC, C), F32)
    sXQ = dint("sXQ", (NKC, P, TLOC), BF16)
    sH = dint("sH", (NIC, P, TLOC), BF16)
    sPart = dint("sPart", (TLOC, C), F32)
    sWDQ = dint("sWDQ", (IFF, C), BF16)
    cc_in = dint("cc_in", (P, C + NKC), F32)
    cc_out = dint("cc_out", (NCORES, P, C + NKC), F32, addr_space="Shared")

    ccd = T.mksem("ccd")
    T.ms = {e: T.mksem("ms_" + e) for e in ENGS}
    T.dq = {"sync": T.mkqueue("qS"), "gpsimd": T.mkqueue("qG"), "scalar": T.mkqueue("qC")}

    def cut(k):
        if _PL is not None and k >= _PL:
            T.stopped = True

    def sb(name, shape, dt):
        return es.enter_context(nc.sbuf_tensor(name, list(shape), dt))

    PA = es.enter_context(nc.psum_tensor("PA", [P, C], F32))
    PB = es.enter_context(nc.psum_tensor("PB", [P, C], F32))

    cst = sb("cst", (P, NCONST), BF16)
    msku = sb("msku", (P, 128), mybir.dt.uint8)
    csf = sb("csf", (P, NCONSTF), F32)
    vec = sb("vec", (P, 11 * NKC), F32)
    gat = sb("gat", (P, 16), F32)
    qbp = sb("qbp", (P, NKC), F32)

    scS = ExitStack()
    S_run = scS.enter_context(nc.sbuf_tensor("S_run", [P, C], F32, side="right"))
    P_bf = scS.enter_context(nc.sbuf_tensor("P_bf", [P, C], BF16, side="right"))
    Tcore = sb("Tcore", (P, NKC), F32)
    wsum2 = sb("wsum2", (P, 3), F32)
    wsA = sb("wsA", (P, 6), F32)
    stm = sb("stm", (P, NKC), F32)

    ID = cst[:, CO_ID:CO_ID + 128]
    UBD = cst[:, CO_UBD:CO_UBD + 128]
    IND4 = cst[:, CO_IND4:CO_IND4 + 4]
    EXP4 = cst[0:4, CO_EXP4:CO_EXP4 + 128]
    ONES4 = cst[0:4, CO_ONES4:CO_ONES4 + 128]
    TRI4 = cst[0:4, CO_TRI4:CO_TRI4 + 4]
    ONEC = cst[:, CO_ONEC:CO_ONEC + 1]
    ZMASK = cst[:, CO_ZERO:CO_ZERO + 128]
    ONEROW_B = cst[0:1, CO_ONES4:CO_ONES4 + 128]
    ONEROW_F = csf[0:1, 2:130]
    IDF = csf[:, 130:258]
    nc.const_aps.register(F32, MAGIC, csf[:, 1:2])

    fbuf = sb("fbuf", (P, C), F32)
    fbuf2 = sb("fbuf2", (P, C), F32)
    fbuf3 = sb("fbuf3", (P, C), F32)
    tmb = sb("tmb", (P, C), BF16)
    tmb2 = sb("tmb2", (P, C), BF16)
    stage = sb("stage", (P, 2048), F32)
    stage2 = sb("stage2", (P, 2048), F32)

    def phase_scope():
        return ExitStack()

    def psb(scope, name, shape, dt):
        return scope.enter_context(nc.sbuf_tensor(name, list(shape), dt))

    T.dma("sync", lambda e: e.dma_start(out=cst[:], in_=consts))
    T.dma("sync", lambda e: e.dma_start(out=msku[:], in_=masku_in))
    T.dma("sync", lambda e: e.dma_start(out=csf[:], in_=constf))
    T.dma("sync", lambda e: e.dma_start(out=vec[:], in_=vecs))
    T.dma("sync", lambda e: e.dma_start(out=gat[:], in_=gates))

    T.dma("sync", lambda e: e.dma_start(out=fbuf[0:1, :], in_=tdrow))
    T.wq("vector", "sync")
    T.op("vector", lambda e: e.tensor_scalar(qbp[:], vec[:, V_QB * NKC:(V_QB + 1) * NKC], 0.125, None, ALU.mult))
    T.op("vector", lambda e: e.memset(S_run[:], 0.0))
    T.op("vector", lambda e: e.memset(Tcore[:], 0.0))
    T.op("vector", lambda e: e.memset(wsum2[:], 0.0))
    T.op("vector", lambda e: e.memset(stm[:], 0.0))

    vsv = T.seal("vector")
    sc0 = phase_scope()
    TD_bc = psb(sc0, "TD_bc", (P, C), F32)
    w_maa1_b = psb(sc0, "w_maa1_b", (P, NKC * 128), BF16)
    w_maa2_lo = psb(sc0, "w_maa2_lo", (64, 2 * C), BF16)
    w_maa2_hi = psb(sc0, "w_maa2_hi", (64, 2 * C), BF16)
    w_td1_b = psb(sc0, "w_td1_b", (P, NKC * DH), BF16)
    w_td2_b = psb(sc0, "w_td2_b", (DH, C), BF16)
    prevc = psb(sc0, "prevc", (P, NKC), BF16)
    mm_lo = psb(sc0, "mm_lo", (64, TT), BF16)
    mm_hi = psb(sc0, "mm_hi", (64, TT), BF16)
    tanh_b = psb(sc0, "tanh_b", (DH, TT), BF16)
    x_tmb = psb(sc0, "x_tmb", (P, C), BF16)
    big_a = psb(sc0, "p0_xfm", (P, NKC * TT), BF16)
    big_b = psb(sc0, "p0_dx", (P, NKC * TT), BF16)
    big_c = psb(sc0, "p0_mix", (P, NKC * TT), BF16)
    big_d = big_c

    T.wait("tensor", vsv[0], vsv[1])
    for j in range(4):
        T.op("tensor", lambda e, j=j: e.matmul(
            PB[:, j * TT:(j + 1) * TT], ONEROW_F, fbuf[0:1, j * TT:(j + 1) * TT],
            start=True, stop=True))
    ts = T.seal("tensor")
    T.wait("vector", ts[0], ts[1])
    T.op("vector", lambda e: e.tensor_copy(TD_bc[:], PB[:]))
    T.bar()

    cut(1)

    stats_blocks = []
    for wi, wap in enumerate((w_gT, w_uT, w_dT)):
        flat = wap.rearrange("a b -> (a b)").rearrange("(n p m) -> n p m", p=P, m=2048)
        for b in range(flat.shape[0]):
            stats_blocks.append((wi, flat[b]))
    st_i = [0]

    st_pend = [None]

    def _stats_reduce(pend):
        buf, wi, tok, col = pend
        use_v = (col % 2 == 1)
        eng = "vector" if use_v else "scalar"
        T.wait(eng, tok[0], tok[1])
        if use_v:
            T.op("vector", lambda e, buf=buf, col=col: e.tensor_reduce(
                fbuf3[:, col:col + 1], buf[:], AX.X, ALU.add, True))
            T.op("vector", lambda e, wi=wi, col=col: e.tensor_add(
                wsum2[:, wi:wi + 1], wsum2[:, wi:wi + 1], fbuf3[:, col:col + 1]))
        else:
            T.op("scalar", lambda e, buf=buf, col=col: e.activation(
                buf[:], buf[:], AF.Abs, accum_out=fbuf3[:, col:col + 1]))
            ssx = T.seal("scalar")
            T.wait("vector", ssx[0], ssx[1])
            T.op("vector", lambda e, wi=wi, col=col: e.tensor_add(
                wsum2[:, wi:wi + 1], wsum2[:, wi:wi + 1], fbuf3[:, col:col + 1]))

    def emit_stats(n, red_eng="any"):
        for _ in range(n):
            i = st_i[0]
            if i >= len(stats_blocks):
                if st_pend[0] is not None:
                    _stats_reduce(st_pend[0])
                    st_pend[0] = None
                return
            wi, blk = stats_blocks[i]
            buf = stage if (i % 2 == 0) else stage2
            T.guard("scalar", "vector")
            tok = T.dma("scalar", lambda e, buf=buf, blk=blk: e.dma_start(out=buf[:], in_=blk))
            if st_pend[0] is not None:
                _stats_reduce(st_pend[0])
            st_pend[0] = (buf, wi, tok, i % 4)
            st_i[0] += 1

    def v3(buf, tlen=TT):
        return buf[:].rearrange("p (kc t) -> p kc t", t=tlen)

    def spill_fm(dst, buf, t0, tlen=TT):
        return lambda e: e.dma_start(
            out=dst.rearrange("kc p t -> p kc t")[:, :, t0:t0 + tlen],
            in_=v3(buf, tlen))

    def load_fm(buf, src, t0, tlen=TT):
        return lambda e: e.dma_start(
            out=v3(buf, tlen),
            in_=src.rearrange("kc p t -> p kc t")[:, :, t0:t0 + tlen])


    T.dma("gpsimd", lambda e: e.dma_start(
        out=w_maa1_b[:].rearrange("p (kc m) -> p kc m", m=128),
        in_=w_maa1.rearrange("(kc p) m -> p kc m", p=P)))
    T.dma("gpsimd", lambda e: e.dma_start(out=w_maa2_lo[0:32, 0:C], in_=w_maa2[0]))
    T.dma("gpsimd", lambda e: e.dma_start(out=w_maa2_lo[32:64, C:2 * C], in_=w_maa2[1]))
    T.dma("gpsimd", lambda e: e.dma_start(out=w_maa2_hi[0:32, 0:C], in_=w_maa2[2]))
    T.dma("gpsimd", lambda e: e.dma_start(out=w_maa2_hi[32:64, C:2 * C], in_=w_maa2[3]))
    T.dma("gpsimd", lambda e: e.dma_start(
        out=w_td1_b[:].rearrange("p (kc m) -> p kc m", m=DH),
        in_=w_td1.rearrange("(kc p) m -> p kc m", p=P)))
    T.dma("gpsimd", lambda e: e.dma_start(out=w_td2_b[:], in_=w_td2))
    T.dma("gpsimd", lambda e: e.dma_start(out=tmb[0:1, :], in_=prevrow))
    T.wq("tensor", "gpsimd")
    for kc in range(NKC):
        T.op("tensor", lambda e, kc=kc: e.matmul(
            PA[:, kc:kc + 1], tmb[0:1, kc * P:(kc + 1) * P], ONEROW_B[0:1, 0:1],
            start=True, stop=True))
    ts = T.seal("tensor")
    T.wait("vector", ts[0], ts[1])
    T.op("vector", lambda e: e.tensor_copy(prevc[:], PA[:, 0:NKC]))
    T.bar()
    cut(1.1)

    for it in range(NTT):
        t0 = it * TT
        for s in range(4):
            r0 = t0 + s * P
            T.guard("gpsimd", "tensor")
            T.dma("gpsimd", lambda e, r0=r0: e.dma_start(out=x_tmb[:], in_=x_in[r0:r0 + P, :]))
            T.wq("tensor", "gpsimd")
            T.guard("tensor", "vector")
            for kc in range(NKC):
                T.op("tensor", lambda e, kc=kc: e.matmul(
                    PA[:, kc * P:(kc + 1) * P], x_tmb[:, kc * P:(kc + 1) * P], ID,
                    start=True, stop=True))
            ts = T.seal("tensor")
            T.wait("vector", ts[0], ts[1])
            T.op("vector", lambda e, s=s: e.tensor_copy(
                v3(big_a)[:, :, s * P:(s + 1) * P],
                PA[:].rearrange("p (kc t) -> p kc t", t=P)))
        cut(1.2 + it / 100)
        xA, dB = v3(big_a), v3(big_b)
        T.op("vector", lambda e: e.tensor_sub(dB[:, :, 1:TT], xA[:, :, 0:TT - 1], xA[:, :, 1:TT]))
        T.op("vector", lambda e: e.tensor_sub(
            dB[:, :, 0:1], prevc[:].broadcast_to([P, NKC, 1]), xA[:, :, 0:1]))
        T.op("vector", lambda e: e.tensor_copy(prevc[:].broadcast_to([P, NKC, 1]), xA[:, :, TT - 1:TT]))
        for kc in range(NKC):
            T.op("vector", lambda e, kc=kc: e.scalar_tensor_tensor(
                big_c[:, kc * TT:(kc + 1) * TT], big_b[:, kc * TT:(kc + 1) * TT],
                vec[:, V_MAAX * NKC + kc:V_MAAX * NKC + kc + 1],
                big_a[:, kc * TT:(kc + 1) * TT], ALU.mult, ALU.add))
        vs = T.seal("vector")
        cut(1.3 + it / 100)
        T.wait("tensor", vs[0], vs[1])
        T.guard("tensor", "scalar")
        import os as _os
        if _os.environ.get("MK_SKIP_MM") == "2":
            T.op("tensor", lambda e: e.matmul(
                PA[:, 0:TT], x_tmb[:, 0:128], x_tmb[:, 0:TT], start=True, stop=True))
        elif _os.environ.get("MK_SKIP_MM") == "3":
            T.op("tensor", lambda e: e.matmul(
                PA[:, 0:TT], w_maa1_b[:, 0:128], x_tmb[:, 0:TT], start=True, stop=True))
        elif _os.environ.get("MK_SKIP_MM") == "4":
            T.op("tensor", lambda e: e.matmul(
                PA[:, 0:TT], x_tmb[:, 0:128], big_c[:, 0:TT], start=True, stop=True))
        elif _os.environ.get("MK_SKIP_MM"):
            T.op("tensor", lambda e: e.matmul(
                PA[:, 0:TT], w_maa1_b[:, 0:128], big_c[:, 0:TT], start=True, stop=True))
        else:
            for ki in range(NKC):
                T.op("tensor", lambda e, ki=ki: e.matmul(
                    PA[:, 0:TT], w_maa1_b[:, ki * 128:(ki + 1) * 128],
                    big_c[:, ki * TT:(ki + 1) * TT], start=(ki == 0), stop=(ki == NKC - 1)))
        ts = T.seal("tensor")
        import os as _os2
        if _os2.environ.get("MK_NO_TANH"):
            T.wait("vector", ts[0], ts[1])
            T.op("vector", lambda e: e.tensor_copy(fbuf2[:, 0:TT], PA[:, 0:TT]))
            T.op("vector", lambda e: e.tensor_copy(fbuf2[:, 0:TT], fbuf2[:, 0:TT]))
        T.wait("scalar", ts[0], ts[1])
        T.op("scalar", lambda e: e.activation(fbuf2[:, 0:TT], PA[:, 0:TT], AF.Tanh)) if not _os2.environ.get("MK_NO_TANH") else None
        ssm = T.seal("scalar")
        T.wait("vector", ssm[0], ssm[1])
        T.op("vector", lambda e: e.tensor_copy(mm_lo[:], fbuf2[0:64, 0:TT]))
        T.op("vector", lambda e: e.tensor_copy(mm_hi[:], fbuf2[64:128, 0:TT]))
        ssm = T.seal("vector")
        T.wait("tensor", ssm[0], ssm[1])
        for r in (0, 1, 2, 3):
            obuf = big_d if r == 3 else big_c
            T.wq("gpsimd", "sync")
            gseals = {}
            for mo in range(NKC):
                off = (mo % 2) * TT + 512
                foff = (mo % 2) * TT
                T.guard("tensor", "vector")
                if mo >= 2:
                    gs2 = gseals[mo - 2]
                    T.wait("vector", gs2[0], gs2[1])
                elif r > 0:
                    T.guard("vector", "gpsimd")
                wsl = (w_maa2_lo if r < 2 else w_maa2_hi)[32 * (r % 2):32 * (r % 2 + 1),
                                                           (r % 2) * C + mo * 128:(r % 2) * C + (mo + 1) * 128]
                msl = (mm_lo if r < 2 else mm_hi)[32 * (r % 2):32 * (r % 2 + 1), :]
                T.op("tensor", lambda e, off=off, wsl=wsl, msl=msl: e.matmul(
                    PA[:, off:off + TT], wsl, msl, start=True, stop=True))
                ts = T.seal("tensor")
                T.wait("vector", ts[0], ts[1])
                T.op("vector", lambda e, r=r, mo=mo, off=off, foff=foff: e.scalar_tensor_tensor(
                    fbuf[:, foff:foff + TT], PA[:, off:off + TT],
                    vec[:, MAA_COL[r] * NKC + mo:MAA_COL[r] * NKC + mo + 1],
                    big_b[:, mo * TT:(mo + 1) * TT], ALU.add, ALU.mult))
                vv = T.seal("vector")
                T.wait("gpsimd", vv[0], vv[1])
                T.op("gpsimd", lambda e, mo=mo, obuf=obuf, foff=foff: e.tensor_add(
                    obuf[:, mo * TT:(mo + 1) * TT], fbuf[:, foff:foff + TT],
                    big_a[:, mo * TT:(mo + 1) * TT]))
                gseals[mo] = T.seal("gpsimd")
            gs = T.seal("gpsimd")
            if r < 3:
                T.wait("sync", gs[0], gs[1])
                dst = (sXR, sXK, sXV)[r]
                T.dma("sync", spill_fm(dst, big_c, t0))
        cut(1.5 + it / 100)
        T.guard("tensor", "gpsimd")
        T.guard("tensor", "vector")
        T.guard("tensor", "scalar")
        for ki in range(NKC):
            T.op("tensor", lambda e, ki=ki: e.matmul(
                PA[0:DH, 0:TT], w_td1_b[:, ki * DH:(ki + 1) * DH],
                big_d[:, ki * TT:(ki + 1) * TT], start=(ki == 0), stop=(ki == NKC - 1)))
        ts = T.seal("tensor")
        T.wait("scalar", ts[0], ts[1])
        T.op("scalar", lambda e: e.activation(tanh_b[:], PA[0:DH, 0:TT], AF.Tanh))
        ssm = T.seal("scalar")
        T.wait("tensor", ssm[0], ssm[1])
        for s in range(4):
            T.guard("tensor", "vector")
            for j in range(4):
                T.op("tensor", lambda e, s=s, j=j: e.matmul(
                    PB[:, j * TT:(j + 1) * TT], tanh_b[:, s * P:(s + 1) * P],
                    w_td2_b[:, j * TT:(j + 1) * TT], start=True, stop=True))
            ts = T.seal("tensor")
            T.wait("vector", ts[0], ts[1])
            T.op("vector", lambda e: e.tensor_add(fbuf2[:], PB[:], TD_bc[:]))
            vv = T.seal("vector")
            T.wait("scalar", vv[0], vv[1])
            T.op("scalar", lambda e: e.activation(fbuf2[:], fbuf2[:], AF.Exp))
            ssm = T.seal("scalar")
            T.wait("vector", ssm[0], ssm[1])
            T.wq("vector", "sync")
            T.op("vector", lambda e: e.tensor_scalar(tmb2[:], fbuf2[:], -1.0, -5.0, ALU.mult, ALU.max))
            vv = T.seal("vector")
            T.wait("sync", vv[0], vv[1])
            T.dma("sync", lambda e, r0=t0 + s * P: e.dma_start(out=sW[r0:r0 + P, :], in_=tmb2[:]))
        cut(1.6 + it / 100)
        emit_stats(8)
        T.bar()

    sc0.close()
    cut(2)

    def tm_projection(tag, wap, xsrc, biasrow_idx, is_k):
        scp = phase_scope()
        wgt = psb(scp, "wgt_" + tag, (P, NKC * C), BF16)
        big_a = psb(scp, "xin_" + tag, (P, NKC * TT), BF16)
        bias_b = psb(scp, "bias_" + tag, (1, C), BF16)
        brow_in = kbrow if biasrow_idx == 0 else vbrow
        T.dma("sync", lambda e: e.dma_start(out=fbuf[0:1, :], in_=brow_in))
        T.wq("vector", "sync")
        T.op("vector", lambda e: e.tensor_copy(bias_b[:], fbuf[0:1, :]))
        T.guard("scalar", "vector")
        T.dma("gpsimd", lambda e: e.dma_start(
            out=wgt[:].rearrange("p (ki co) -> p ki co", co=C),
            in_=wap.rearrange("(ki p) co -> p ki co", p=P)))
        T.wq("tensor", "gpsimd")
        T.guard("tensor", "vector")
        for it in range(NTT):
            t0 = it * TT
            T.guard("sync", "tensor")
            T.dma("sync", load_fm(big_a, xsrc, t0))
            T.wq("tensor", "sync")
            for s in range(4):
                r0 = t0 + s * P
                pp = PA if s % 2 == 0 else PB
                T.guard("tensor", "vector")
                for co in range(4):
                    for ki in range(NKC):
                        T.op("tensor", lambda e, s=s, co=co, ki=ki, pp=pp: e.matmul(
                            pp[:, co * TT:(co + 1) * TT],
                            big_a[:, ki * TT + s * P: ki * TT + (s + 1) * P],
                            wgt[:, ki * C + co * TT: ki * C + (co + 1) * TT],
                            start=(ki == 0), stop=False))
                for co in range(4):
                    T.op("tensor", lambda e, co=co, pp=pp: e.matmul(
                        pp[:, co * TT:(co + 1) * TT], ONEROW_B,
                        bias_b[0:1, co * TT:(co + 1) * TT], start=False, stop=True))
                ts = T.seal("tensor")
                T.wq("vector", "sync")
                if is_k:
                    T.guard("sync", "scalar")
                    T.dma("sync", lambda e, r0=r0: e.dma_start(out=tmb[:], in_=sW[r0:r0 + P, :]))
                    T.wq("scalar", "sync")
                    T.op("scalar", lambda e: e.activation(fbuf[:], tmb[:], AF.Exp))
                    ss = T.seal("scalar")
                    T.wait("vector", ss[0], ss[1])
                    T.op("vector", lambda e: e.tensor_scalar(fbuf[:], fbuf[:], -1.0, 1.0, ALU.mult, ALU.add))
                    T.wait("vector", ts[0], ts[1])
                    T.op("vector", lambda e, pp=pp: e.tensor_mul(tmb2[:], pp[:], fbuf[:]))
                else:
                    T.wait("vector", ts[0], ts[1])
                    T.op("vector", lambda e, pp=pp: e.tensor_copy(tmb2[:], pp[:]))
                vv = T.seal("vector")
                T.wait("sync", vv[0], vv[1])
                dst = sK0 if is_k else sV
                T.dma("sync", lambda e, r0=r0, dst=dst: e.dma_start(out=dst[r0:r0 + P, :], in_=tmb2[:]))
            emit_stats(8)
        T.bar()
        scp.close()

    tm_projection("k", w_k, sXK, 0, True)
    cut(3)
    tm_projection("v", w_v, sXV, 1, False)
    cut(4)

    def scan_bufs(scope, tag, scan):
        B = {}
        B["S4b"] = psb(scope, "S4b" + tag, (4, C), BF16)
        B["S4n"] = psb(scope, "S4n" + tag, (4, C), BF16)
        B["S4h"] = psb(scope, "S4h" + tag, (4, C), BF16)
        B["Bx4"] = psb(scope, "Bx4" + tag, (4, C), BF16)
        B["T_fm"] = psb(scope, "T_fm" + tag, (P, NKC), F32)
        B["eT"] = psb(scope, "eT" + tag, (P, NKC), F32)
        B["EnT"] = psb(scope, "EnT" + tag, (P, C), BF16)
        B["Ktil"] = psb(scope, "Ktil" + tag, (P, C), BF16)
        B["wrow"] = [psb(scope, "wrow0" + tag, (P, C), BF16), psb(scope, "wrow1" + tag, (P, C), BF16)]
        B["krow"] = [psb(scope, "krow0" + tag, (P, C), BF16), psb(scope, "krow1" + tag, (P, C), BF16)]
        B["vrow"] = [psb(scope, "vrow0" + tag, (P, C), BF16), psb(scope, "vrow1" + tag, (P, C), BF16)]
        if scan:
            B["Kbar"] = psb(scope, "Kbar" + tag, (P, C), BF16)
            B["k0f"] = psb(scope, "k0f" + tag, (P, C), BF16)
            B["Esub"] = psb(scope, "Esub" + tag, (P, C), BF16)
            B["Ensb"] = psb(scope, "Ensb" + tag, (P, C), BF16)
            B["Eful"] = psb(scope, "Eful" + tag, (P, C), BF16)
            B["Eq32"] = psb(scope, "Eq32" + tag, (P, C), BF16)
            B["Qtl"] = psb(scope, "Qtl" + tag, (P, C), BF16)
            B["Qht"] = psb(scope, "Qht" + tag, (P, C), BF16)
            B["Q32"] = psb(scope, "Q32" + tag, (P, C), BF16)
            B["Khat"] = psb(scope, "Khat" + tag, (P, C), BF16)
            B["Asb"] = psb(scope, "Asb" + tag, (P, 512), BF16)
            B["yb"] = psb(scope, "yb" + tag, (P, C), BF16)
            B["q512"] = psb(scope, "q512" + tag, (P, NKC * TT), BF16)
            B["D32"] = psb(scope, "D32" + tag, (P, 4 * NKC), F32)
            B["M1"] = psb(scope, "M1" + tag, (P, C), BF16)
            B["M2"] = psb(scope, "M2" + tag, (P, C), BF16)
            B["M3"] = psb(scope, "M3" + tag, (P, C), BF16)
        return B

    ch_toks = {}

    def chunk_loads(B, j, scan):
        r0 = j * CH
        par = j % 2
        toks = []
        toks.append(T.dma("sync", lambda e, r0=r0, B=B, par=par: e.dma_start(
            out=B["wrow"][par][:], in_=sW[r0:r0 + P, :])))
        toks.append(T.dma("sync", lambda e, r0=r0, B=B, par=par: e.dma_start(
            out=B["krow"][par][:], in_=sK0[r0:r0 + P, :])))
        toks.append(T.dma("sync", lambda e, r0=r0, B=B, par=par: e.dma_start(
            out=B["vrow"][par][:], in_=sV[r0:r0 + P, :])))
        ch_toks[j] = toks

    q_toks = {}

    def q512_load(B, g):
        T.guard("sync", "vector")
        q_toks[g] = T.dma("sync", load_fm(B["q512"], sQ, g * TT))

    def chunk_wait(j, scan=False):
        toks = list(ch_toks[j])
        if scan and j % 4 == 0:
            toks.append(q_toks[j // 4])
        for tok in toks:
            T.wait("tensor", tok[0], tok[1])
            T.wait("vector", tok[0], tok[1])
            T.wait("scalar", tok[0], tok[1])

    def chunk_prep(B, j, scan):
        par = j % 2
        wr, kr = B["wrow"][par], B["krow"][par]
        """32-block machinery: S32 sums, prefix, decay refs, FM/TM factors."""
        S4b, S4n, S4h, Bx4 = B["S4b"], B["S4n"], B["S4h"], B["Bx4"]
        T_fm, eT, EnT, Ktil = B["T_fm"], B["eT"], B["EnT"], B["Ktil"]
        T.guard("tensor", "vector")
        T.guard("tensor", "scalar")
        for n in range(4):
            T.op("tensor", lambda e, n=n, wr=wr, kr=kr: e.matmul(
                PA[0:4, n * TT:(n + 1) * TT], IND4, wr[:, n * TT:(n + 1) * TT],
                start=True, stop=True))
        for kc in range(NKC):
            T.op("tensor", lambda e, kc=kc, wr=wr, kr=kr: e.matmul(
                PB[:, kc:kc + 1], wr[:, kc * P:(kc + 1) * P], ONEC,
                start=True, stop=True))
        ts = T.seal("tensor")
        T.wait("vector", ts[0], ts[1])
        T.op("vector", lambda e, S4b=S4b: e.tensor_copy(S4b[:], PA[0:4, :]))
        T.op("vector", lambda e, S4b=S4b, S4n=S4n: e.tensor_scalar(S4n[:], S4b[:], -1.0, None, ALU.mult))
        T.op("vector", lambda e, S4b=S4b, S4h=S4h: e.tensor_scalar(S4h[:], S4b[:], -0.5, None, ALU.mult))
        T.op("vector", lambda e, T_fm=T_fm: e.tensor_copy(T_fm[:], PB[:, 0:NKC]))
        vv = T.seal("vector")
        T.wait("scalar", vv[0], vv[1])
        T.op("scalar", lambda e, T_fm=T_fm, eT=eT: e.activation(eT[:], T_fm[:], AF.Exp))
        T.wait("tensor", vv[0], vv[1])
        for n in range(4):
            T.op("tensor", lambda e, n=n, S4b=S4b: e.matmul(
                PA[0:4, n * TT:(n + 1) * TT], TRI4, S4b[0:4, n * TT:(n + 1) * TT],
                start=True, stop=True))
        ts = T.seal("tensor")
        T.wait("vector", ts[0], ts[1])
        T.op("vector", lambda e, Bx4=Bx4: e.tensor_copy(Bx4[:], PA[0:4, :]))
        vv = T.seal("vector")
        T.wait("tensor", vv[0], vv[1])
        T.guard("tensor", "scalar")
        for n in range(4):
            T.op("tensor", lambda e, n=n, wr=wr, kr=kr: e.matmul(
                PB[:, n * TT:(n + 1) * TT], UBD, wr[:, n * TT:(n + 1) * TT],
                start=True, stop=False))
            T.op("tensor", lambda e, n=n, Bx4=Bx4: e.matmul(
                PB[:, n * TT:(n + 1) * TT], EXP4, Bx4[0:4, n * TT:(n + 1) * TT],
                start=False, stop=False))
            T.op("tensor", lambda e, n=n, S4n=S4n: e.matmul(
                PB[:, n * TT:(n + 1) * TT], ONES4, S4n[0:4, n * TT:(n + 1) * TT],
                start=False, stop=True))
        ts = T.seal("tensor")
        T.wait("scalar", ts[0], ts[1])
        T.op("scalar", lambda e, EnT=EnT: e.activation(EnT[:], PB[:], AF.Exp, scale=-1.0))
        ss = T.seal("scalar")
        T.wait("vector", ss[0], ss[1])
        T.op("vector", lambda e, Ktil=Ktil, kr=kr: e.tensor_mul(Ktil[:], kr[:], EnT[:]))

        if scan:
            Kbar, k0f = B["Kbar"], B["k0f"]
            Esub, Ensb, Eful, Eq32 = B["Esub"], B["Ensb"], B["Eful"], B["Eq32"]
            Qtl, Qht, Q32, Khat, q512 = B["Qtl"], B["Qht"], B["Q32"], B["Khat"], B["q512"]
            D32 = B["D32"]
            T.guard("tensor", "scalar")
            for n in range(4):
                T.op("tensor", lambda e, n=n, wr=wr, kr=kr: e.matmul(
                    PB[:, n * TT:(n + 1) * TT], UBD, wr[:, n * TT:(n + 1) * TT],
                    start=True, stop=False))
                T.op("tensor", lambda e, n=n, S4n=S4n: e.matmul(
                    PB[:, n * TT:(n + 1) * TT], EXP4, S4n[0:4, n * TT:(n + 1) * TT],
                    start=False, stop=True))
            ts = T.seal("tensor")
            T.wait("scalar", ts[0], ts[1])
            T.op("scalar", lambda e: e.activation(fbuf[:], PB[:], AF.Exp, scale=-1.0))
            ss = T.seal("scalar")
            T.wait("vector", ss[0], ss[1])
            T.op("vector", lambda e, Kbar=Kbar, kr=kr: e.tensor_mul(Kbar[:], kr[:], fbuf[:]))
            T.guard("tensor", "vector")
            for kc in range(NKC):
                T.op("tensor", lambda e, kc=kc, wr=wr, kr=kr: e.matmul(
                    PA[:, kc * P:(kc + 1) * P], kr[:, kc * P:(kc + 1) * P], ID,
                    start=True, stop=True))
            ts = T.seal("tensor")
            T.wait("vector", ts[0], ts[1])
            T.op("vector", lambda e, k0f=k0f: e.tensor_copy(k0f[:], PA[:]))
            vv = T.seal("vector")
            T.wait("tensor", vv[0], vv[1])
            for kc in range(NKC):
                T.op("tensor", lambda e, kc=kc, wr=wr, kr=kr: e.matmul(
                    PA[:, kc * P:(kc + 1) * P], wr[:, kc * P:(kc + 1) * P], UBD,
                    start=True, stop=False))
                T.op("tensor", lambda e, kc=kc, S4h=S4h: e.matmul(
                    PA[:, kc * P:(kc + 1) * P], S4h[0:4, kc * P:(kc + 1) * P], EXP4,
                    start=False, stop=True))
            ts = T.seal("tensor")
            T.wait("scalar", ts[0], ts[1])
            T.op("scalar", lambda e, Esub=Esub: e.activation(Esub[:], PA[:], AF.Exp))
            T.op("scalar", lambda e, Ensb=Ensb: e.activation(Ensb[:], PA[:], AF.Exp, scale=-1.0))
            ss = T.seal("scalar")
            T.guard("tensor", "scalar")
            T.guard("tensor", "vector")
            for kc in range(NKC):
                T.op("tensor", lambda e, kc=kc, wr=wr, kr=kr: e.matmul(
                    PB[:, kc * P:(kc + 1) * P], wr[:, kc * P:(kc + 1) * P], UBD,
                    start=True, stop=True))
            ts = T.seal("tensor")
            T.wait("scalar", ts[0], ts[1])
            T.op("scalar", lambda e, Eq32=Eq32: e.activation(Eq32[:], PB[:], AF.Exp))
            ss = T.seal("scalar")
            T.wait("tensor", ss[0], ss[1])
            for kc in range(NKC):
                T.op("tensor", lambda e, kc=kc, wr=wr, kr=kr: e.matmul(
                    PB[:, kc * P:(kc + 1) * P], wr[:, kc * P:(kc + 1) * P], UBD,
                    start=True, stop=False))
                T.op("tensor", lambda e, kc=kc, Bx4=Bx4: e.matmul(
                    PB[:, kc * P:(kc + 1) * P], Bx4[0:4, kc * P:(kc + 1) * P], EXP4,
                    start=False, stop=True))
            ts = T.seal("tensor")
            T.wait("scalar", ts[0], ts[1])
            T.op("scalar", lambda e, Eful=Eful: e.activation(Eful[:], PB[:], AF.Exp))
            ss = T.seal("scalar")
            T.guard("tensor", "scalar")
            for kc in range(NKC):
                T.op("tensor", lambda e, kc=kc, S4b=S4b: e.matmul(
                    PA[:, 1024 + kc * 4: 1024 + (kc + 1) * 4],
                    S4b[0:4, kc * P:(kc + 1) * P], ID[0:4, 0:4],
                    start=True, stop=True))
            ts = T.seal("tensor")
            T.wait("scalar", ts[0], ts[1])
            T.op("scalar", lambda e, D32=D32: e.activation(D32[:], PA[:, 1024:1024 + 4 * NKC], AF.Exp))
            ss = T.seal("scalar")
            qv = q512[:].rearrange("p (kc t) -> p kc t", t=TT)
            j4 = j % 4
            T.wait("vector", ss[0], ss[1])
            for nm, Efac in (("Qtl", Esub), ("Qht", Eful), ("Q32", Eq32)):
                dst = B[nm]
                T.op("vector", lambda e, dst=dst, Efac=Efac, qv=qv, j4=j4: e.tensor_tensor(
                    dst[:].rearrange("p (kc t) -> p kc t", t=CH),
                    qv[:, :, j4 * CH:(j4 + 1) * CH],
                    Efac[:].rearrange("p (kc t) -> p kc t", t=CH), ALU.mult))
            T.op("vector", lambda e, Khat=Khat, k0f=k0f, Ensb=Ensb: e.tensor_mul(Khat[:], k0f[:], Ensb[:]))

    def state_update(B, j):
        Ktil, eT, T_fm = B["Ktil"], B["eT"], B["T_fm"]
        vrow = B["vrow"][j % 2]
        T.guard("tensor", "vector")
        for sp in range(NKC):
            T.op("tensor", lambda e, sp=sp, Ktil=Ktil, vrow=vrow: e.matmul(
                PA[:, sp * P:(sp + 1) * P], Ktil[:, sp * P:(sp + 1) * P],
                vrow[:, sp * P:(sp + 1) * P], start=True, stop=True))
        ts = T.seal("tensor")
        T.wait("vector", ts[0], ts[1])
        for sp in range(NKC):
            T.op("vector", lambda e, sp=sp, eT=eT: e.scalar_tensor_tensor(
                S_run[:, sp * P:(sp + 1) * P], S_run[:, sp * P:(sp + 1) * P],
                eT[:, sp:sp + 1], PA[:, sp * P:(sp + 1) * P], ALU.mult, ALU.add))
        T.op("vector", lambda e, T_fm=T_fm: e.tensor_add(Tcore[:], Tcore[:], T_fm[:]))

    sc4 = phase_scope()
    B4s = scan_bufs(sc4, "_s", scan=False)
    chunk_loads(B4s, 0, scan=False)
    for j in range(NCH):
        if j + 1 < NCH:
            if j >= 1:
                T.guard("sync", "tensor")
                T.guard("sync", "vector")
            chunk_loads(B4s, j + 1, scan=False)
        chunk_wait(j)
        chunk_prep(B4s, j, scan=False)
        state_update(B4s, j)
        emit_stats(7)
    vv = T.seal("vector")
    T.wait("sync", vv[0], vv[1])
    T.dma("sync", lambda e: e.dma_start(out=cc_in[:, 0:C], in_=S_run[:]))
    T.dma("sync", lambda e: e.dma_start(out=cc_in[:, C:C + NKC], in_=Tcore[:]))
    T.wq("gpsimd", "sync")
    T.op("gpsimd", lambda e: e.collective_compute(
        "AllGather", ALU.bypass, replica_groups=[list(range(NCORES))],
        ins=[cc_in], outs=[cc_out]), inc=(ccd, 1))
    T.bar()
    sc4.close()
    cut(5)

    emit_stats(len(stats_blocks) + 1, "vector")
    T.bar()

    for wi in range(3):
        T.guard("tensor", "vector")
        T.op("tensor", lambda e, wi=wi: e.matmul(
            PA[0:1, 0:128], wsum2[:, wi:wi + 1], IDF, start=True, stop=True))
        ts = T.seal("tensor")
        T.wait("vector", ts[0], ts[1])
        T.op("vector", lambda e: e.tensor_reduce(fbuf[0:1, 0:1], PA[0:1, 0:128], AX.X, ALU.add))
        T.op("vector", lambda e: e.tensor_scalar(
            fbuf[0:1, 0:1], fbuf[0:1, 0:1], 1.0 / (C * IFF), QEPS, ALU.mult, ALU.add))
        T.op("vector", lambda e: e.reciprocal(fbuf[0:1, 1:2], fbuf[0:1, 0:1]))
        vv = T.seal("vector")
        T.wait("tensor", vv[0], vv[1])
        T.op("tensor", lambda e: e.matmul(PB[:, 0:1], ONEROW_F, fbuf[0:1, 0:1], start=True, stop=True))
        T.op("tensor", lambda e: e.matmul(PB[:, 1:2], ONEROW_F, fbuf[0:1, 1:2], start=True, stop=True))
        ts = T.seal("tensor")
        T.wait("vector", ts[0], ts[1])
        T.op("vector", lambda e, wi=wi: e.tensor_copy(wsA[:, wi:wi + 1], PB[:, 0:1]))
        T.op("vector", lambda e, wi=wi: e.tensor_copy(wsA[:, 3 + wi:4 + wi], PB[:, 1:2]))
    T.bar()

    wdq_i = [0]
    wdq_ld = [0]
    wdq_ldtok = {}
    wdq_sptok = {}
    wdqb = [scS.enter_context(nc.sbuf_tensor("wdqb0", [P, C], BF16, side="right")),
            scS.enter_context(nc.sbuf_tensor("wdqb1", [P, C], BF16, side="right"))]
    wdqs = [stage, stage2]

    def wdq_load(k):
        while wdq_ld[0] < min(k, NIC):
            ic = wdq_ld[0]
            T.guard("gpsimd", "vector")
            wdq_ldtok[ic] = T.dma("gpsimd", lambda e, ic=ic: e.dma_start(
                out=wdqs[ic % 2][:], in_=w_dT[ic * P:(ic + 1) * P, :]))
            wdq_ld[0] += 1

    def emit_wdq(n):
        for _ in range(n):
            ic = wdq_i[0]
            if ic >= NIC:
                return
            wdq_load(ic + 2)
            buf = wdqs[ic % 2]
            tok = wdq_ldtok[ic]
            T.wait("scalar", tok[0], tok[1])
            T.op("scalar", lambda e, buf=buf: e.activation(
                buf[:], buf[:], AF.Identity, scale=wsA[:, 5:6], bias=MAGIC))
            ss = T.seal("scalar")
            T.wait("vector", ss[0], ss[1])
            T.op("vector", lambda e, buf=buf: e.tensor_scalar(
                buf[:], buf[:], MAGIC - 1.0, MAGIC + 1.0, ALU.max, ALU.min))
            if ic >= 2:
                st = wdq_sptok[ic - 2]
                T.wait("vector", st[0], st[1])
            T.op("vector", lambda e, ic=ic, buf=buf: e.tensor_scalar_add(
                wdqb[ic % 2][:], buf[:], -MAGIC))
            T.guard("gpsimd", "vector")
            wdq_sptok[ic] = T.dma("gpsimd", lambda e, ic=ic: e.dma_start(
                out=sWDQ[ic * P:(ic + 1) * P, :], in_=wdqb[ic % 2][:]))
            wdq_i[0] += 1

    sc1 = phase_scope()
    qw_s = psb(sc1, "qw_s", (P, NKC * C), BF16)
    qx_s = psb(sc1, "qx_s", (P, NKC * TT), BF16)
    qo_s = psb(sc1, "qo_s", (P, NKC * TT), BF16)
    T.dma("gpsimd", lambda e: e.dma_start(
        out=qw_s[:].rearrange("p (ki co) -> p ki co", co=C),
        in_=w_q.rearrange("(ki p) co -> p ki co", p=P)))
    T.wq("tensor", "gpsimd")
    for it in range(NTT):
        t0 = it * TT
        T.guard("sync", "tensor")
        T.dma("sync", load_fm(qx_s, sXR, t0))
        T.wq("tensor", "sync")
        for mo in range(NKC):
            pp, off = (PA, (mo % 4) * TT) if mo % 8 < 4 else (PB, (mo % 4) * TT)
            if mo >= 8:
                T.guard("tensor", "scalar")
            for ki in range(NKC):
                T.op("tensor", lambda e, mo=mo, ki=ki, pp=pp, off=off: e.matmul(
                    pp[:, off:off + TT], qw_s[:, ki * C + mo * 128: ki * C + (mo + 1) * 128],
                    qx_s[:, ki * TT:(ki + 1) * TT], start=(ki == 0), stop=(ki == NKC - 1)))
            ts = T.seal("tensor")
            T.wait("scalar", ts[0], ts[1])
            if mo == 0:
                T.wq("scalar", "sync")
            T.op("scalar", lambda e, mo=mo, pp=pp, off=off: e.activation(
                qo_s[:, mo * TT:(mo + 1) * TT], pp[:, off:off + TT], AF.Identity,
                bias=qbp[:, mo:mo + 1], scale=0.125))
        ss = T.seal("scalar")
        T.wait("sync", ss[0], ss[1])
        T.dma("sync", spill_fm(sQ, qo_s, t0))
    T.bar()
    sc1.close()
    cut(6)

    T.wait("sync", ccd, 1)
    T.op("vector", lambda e: e.memset(S_run[:], 0.0))
    for i in range(NCORES - 1):
        buf = stage if i % 2 == 0 else stage2
        T.guard("sync", "vector")
        T.guard("sync", "scalar")
        T.dma("sync", lambda e, i=i, buf=buf: e.dma_start(
            out=buf[:], in_=cc_out[i, :, 0:C]))
        T.dma("sync", lambda e, i=i: e.dma_start(out=fbuf[:, 0:NKC], in_=cc_out[i, :, C:C + NKC]))
        T.wq("scalar", "sync")
        T.op("scalar", lambda e: e.activation(fbuf2[:, 0:NKC], fbuf[:, 0:NKC], AF.Exp))
        ss = T.seal("scalar")
        T.wait("vector", ss[0], ss[1])
        T.op("vector", lambda e, i=i: e.tensor_scalar(
            fbuf2[:, NKC:2 * NKC], fbuf2[:, 0:NKC], gat[:, i:i + 1], gat[:, 8 + i:8 + i + 1],
            ALU.mult, ALU.add))
        T.op("vector", lambda e, i=i, buf=buf: e.tensor_scalar_mul(buf[:], buf[:], gat[:, i:i + 1]))
        for sp in range(NKC):
            T.op("vector", lambda e, sp=sp, buf=buf: e.scalar_tensor_tensor(
                S_run[:, sp * P:(sp + 1) * P], S_run[:, sp * P:(sp + 1) * P],
                fbuf2[:, NKC + sp:NKC + sp + 1], buf[:, sp * P:(sp + 1) * P],
                ALU.mult, ALU.add))
    T.op("vector", lambda e: e.tensor_copy(P_bf[:], S_run[:]))
    T.bar()
    cut(7)

    sc6 = phase_scope()
    B6 = scan_bufs(sc6, "_g", scan=True)
    chunk_loads(B6, 0, scan=True)
    q512_load(B6, 0)
    wdq_load(2)
    for j in range(NCH):
        if j + 1 < NCH:
            if j >= 1:
                T.guard("sync", "tensor")
                T.guard("sync", "vector")
            chunk_loads(B6, j + 1, scan=True)
        chunk_wait(j, scan=True)
        chunk_prep(B6, j, scan=True)
        if (j + 1) % 4 == 0 and j + 1 < NCH:
            q512_load(B6, (j + 1) // 4)
        Kbar6, Q326 = B6["Kbar"], B6["Q32"]
        vrow6 = B6["vrow"][j % 2]
        D326 = B6["D32"]
        Ms = [B6["M1"], B6["M2"], B6["M3"]]
        T.guard("tensor", "vector")
        T.guard("tensor", "scalar")
        for sig in range(3):
            if sig > 0:
                T.guard("tensor", "vector")
            for sp in range(NKC):
                tp = (sig * SUB, 0)
                T.op("tensor", lambda e, sig=sig, sp=sp, Kbar6=Kbar6, tp=tp, vrow6=vrow6: e.matmul(
                    PA[:, sp * P:(sp + 1) * P],
                    Kbar6[sig * SUB:(sig + 1) * SUB, sp * P:(sp + 1) * P],
                    vrow6[sig * SUB:(sig + 1) * SUB, sp * P:(sp + 1) * P],
                    start=True, stop=True, tile_position=tp))
            ts = T.seal("tensor")
            T.wait("vector", ts[0], ts[1])
            if sig == 0:
                T.op("vector", lambda e: e.tensor_copy(Ms[0][:], PA[:]))
            else:
                for sp in range(NKC):
                    T.op("vector", lambda e, sig=sig, sp=sp: e.scalar_tensor_tensor(
                        Ms[sig][:, sp * P:(sp + 1) * P], Ms[sig - 1][:, sp * P:(sp + 1) * P],
                        D326[:, sp * 4 + sig - 1:sp * 4 + sig], PA[:, sp * P:(sp + 1) * P],
                        ALU.mult, ALU.add))
        mv = T.seal("vector")
        Khat, Qtl, Qht, Asb, yb = B6["Khat"], B6["Qtl"], B6["Qht"], B6["Asb"], B6["yb"]
        hseal6 = {}
        tseal6 = {}
        T.wait("tensor", mv[0], mv[1])
        for h in range(32):
            sp = h // 2
            pb_ = (h % 2) * DH
            aoff = (h % 4) * 128
            ksl = Khat[pb_:pb_ + DH, sp * 128:(sp + 1) * 128]
            qsl2 = Qtl[pb_:pb_ + DH, sp * 128:(sp + 1) * 128]
            if h >= 4:
                vs4 = hseal6.get(h - 4)
                T.wait("tensor", vs4[0], vs4[1])
                ts4 = tseal6.get(h - 4)
                T.wait("vector", ts4[0], ts4[1])
            T.op("tensor", lambda e, ksl=ksl, qsl2=qsl2, aoff=aoff, pb_=pb_: e.matmul(
                PA[:, aoff:aoff + 128], ksl, qsl2, start=True, stop=True,
                tile_position=(pb_, 0)))
            ts = T.seal("tensor")
            T.wait("vector", ts[0], ts[1])
            T.op("vector", lambda e, aoff=aoff: e.memset(Asb[:, aoff:aoff + 128], 0.0))
            T.op("vector", lambda e, aoff=aoff: e.copy_predicated(
                Asb[:, aoff:aoff + 128], msku[:], PA[:, aoff:aoff + 128]))
            vv = T.seal("vector")
            hseal6[h] = vv
            T.wait("tensor", vv[0], vv[1])
            T.op("tensor", lambda e, h=h, aoff=aoff, vrow6=vrow6: e.matmul(
                PB[:, h * DH:(h + 1) * DH], Asb[:, aoff:aoff + 128],
                vrow6[:, h * DH:(h + 1) * DH], start=True, stop=False))
            for rho in range(1, 4):
                T.op("tensor", lambda e, h=h, sp=sp, pb_=pb_, rho=rho, Q326=Q326: e.matmul(
                    PB[rho * SUB:(rho + 1) * SUB, h * DH:(h + 1) * DH],
                    Q326[pb_:pb_ + DH, sp * 128 + rho * SUB: sp * 128 + (rho + 1) * SUB],
                    Ms[rho - 1][pb_:pb_ + DH, sp * 128 + pb_: sp * 128 + pb_ + DH],
                    start=False, stop=False, tile_position=(pb_, rho * SUB)))
            T.op("tensor", lambda e, h=h, sp=sp, pb_=pb_, Qht=Qht: e.matmul(
                PB[:, h * DH:(h + 1) * DH], Qht[pb_:pb_ + DH, sp * 128:(sp + 1) * 128],
                P_bf[pb_:pb_ + DH, sp * 128 + pb_: sp * 128 + pb_ + DH],
                start=False, stop=True, tile_position=(pb_, 0)))
            tseal6[h] = T.seal("tensor")
            if h % 8 == 7:
                emit_wdq(1)
        ts = T.seal("tensor")
        T.wait("vector", ts[0], ts[1])
        T.wq("vector", "sync")
        T.op("vector", lambda e, yb=yb: e.tensor_copy(yb[:], PB[:]))
        vv = T.seal("vector")
        T.wait("sync", vv[0], vv[1])
        T.dma("sync", lambda e, j=j, yb=yb: e.dma_start(out=sY[j * CH:(j + 1) * CH, :], in_=yb[:]))
        state_update(B6, j)
        T.op("vector", lambda e: e.tensor_copy(P_bf[:], S_run[:]))
    emit_wdq(NIC)
    T.bar()
    sc6.close()
    scS.close()
    cut(8)

    sc6b = phase_scope()
    ow_s = psb(sc6b, "wgt_o", (P, NKC * C), BF16)
    zg_s = psb(sc6b, "zg6b", (P, C), BF16)
    xqf_s = psb(sc6b, "xqf6b", (P, C), BF16)
    T.dma("gpsimd", lambda e: e.dma_start(
        out=ow_s[:].rearrange("p (ki co) -> p ki co", co=C),
        in_=w_o.rearrange("(ki p) co -> p ki co", p=P)))
    lnb_bf = psb(sc6b, "lnb_bf", (P, NKC), BF16)
    T.op("vector", lambda e: e.tensor_copy(lnb_bf[:], vec[:, V_LNB * NKC:(V_LNB + 1) * NKC]))
    vv = T.seal("vector")
    T.wq("tensor", "gpsimd")
    T.wait("tensor", vv[0], vv[1])
    for ki in range(NKC):
        for n4 in range(4):
            T.op("tensor", lambda e, ki=ki, n4=n4: e.matmul(
                PA[0:1, n4 * TT:(n4 + 1) * TT], lnb_bf[:, ki:ki + 1],
                ow_s[:, ki * C + n4 * TT: ki * C + (n4 + 1) * TT],
                start=(ki == 0), stop=(ki == NKC - 1)))
    ts = T.seal("tensor")
    T.wait("vector", ts[0], ts[1])
    T.op("vector", lambda e: e.tensor_copy(tmb[0:1, :], PA[0:1, 0:C]))
    vv = T.seal("vector")
    T.wait("tensor", vv[0], vv[1])
    for n in range(4):
        T.op("tensor", lambda e, n=n: e.matmul(
            PA[:, n * TT:(n + 1) * TT], ONEROW_B, tmb[0:1, n * TT:(n + 1) * TT],
            start=True, stop=True))
    ts = T.seal("tensor")
    T.wait("vector", ts[0], ts[1])
    T.op("vector", lambda e: e.tensor_copy(fbuf3[:], PA[:]))
    T.bar()

    for s in range(NCH):
        r0 = s * CH
        T.guard("sync", "vector")
        T.guard("sync", "scalar")
        T.dma("sync", lambda e, r0=r0: e.dma_start(out=tmb[:], in_=sY[r0:r0 + P, :]))
        T.dma("sync", lambda e, r0=r0: e.dma_start(out=fbuf[:], in_=x_in[r0:r0 + P, :]))
        T.wq("vector", "sync")
        T.wq("scalar", "sync")
        T.op("vector", lambda e: e.tensor_reduce(fbuf2[:, 0:1], tmb[:], AX.X, ALU.add))
        T.op("scalar", lambda e: e.activation(stage[:], tmb[:], AF.Square, accum_out=fbuf2[:, 1:2]))
        ss = T.seal("scalar")
        T.wait("vector", ss[0], ss[1])
        T.op("vector", lambda e: e.tensor_scalar_mul(fbuf2[:, 0:1], fbuf2[:, 0:1], 1.0 / C))
        T.op("vector", lambda e: e.tensor_scalar_mul(fbuf2[:, 1:2], fbuf2[:, 1:2], 1.0 / C))
        T.op("vector", lambda e: e.scalar_tensor_tensor(
            fbuf2[:, 2:3], fbuf2[:, 0:1], -1.0, fbuf2[:, 0:1], ALU.mult, ALU.mult))
        T.op("vector", lambda e: e.tensor_add(fbuf2[:, 1:2], fbuf2[:, 1:2], fbuf2[:, 2:3]))
        T.op("vector", lambda e: e.tensor_scalar_add(fbuf2[:, 1:2], fbuf2[:, 1:2], LNEPS))
        T.op("vector", lambda e: e.reciprocal(fbuf2[:, 1:2], fbuf2[:, 1:2]))
        vv = T.seal("vector")
        T.wait("scalar", vv[0], vv[1])
        T.op("scalar", lambda e: e.sqrt(fbuf2[:, 3:4], fbuf2[:, 1:2]))
        T.op("scalar", lambda e: e.activation(fbuf2[:, 4:5], fbuf2[:, 0:1], AF.Identity, scale=-1.0))
        T.op("scalar", lambda e: e.activation(tmb2[:], tmb[:], AF.Identity, bias=fbuf2[:, 4:5]))
        ss = T.seal("scalar")
        T.guard("tensor", "scalar")
        T.wait("tensor", ss[0], ss[1])
        for kc in range(NKC):
            T.op("tensor", lambda e, kc=kc: e.matmul(
                PA[:, kc * P:(kc + 1) * P], tmb2[:, kc * P:(kc + 1) * P], ID,
                start=True, stop=True))
        ts = T.seal("tensor")
        T.wait("scalar", ts[0], ts[1])
        for kc in range(NKC):
            T.op("scalar", lambda e, kc=kc: e.activation(
                zg_s[:, kc * P:(kc + 1) * P], PA[:, kc * P:(kc + 1) * P], AF.Identity,
                scale=vec[:, V_LNG * NKC + kc:V_LNG * NKC + kc + 1]))
        ss = T.seal("scalar")
        T.guard("tensor", "vector")
        T.wait("tensor", ss[0], ss[1])
        for co in range(4):
            for ki in range(NKC):
                T.op("tensor", lambda e, co=co, ki=ki: e.matmul(
                    PB[:, co * TT:(co + 1) * TT], zg_s[:, ki * P:(ki + 1) * P],
                    ow_s[:, ki * C + co * TT: ki * C + (co + 1) * TT],
                    start=(ki == 0), stop=(ki == NKC - 1)))
        ts = T.seal("tensor")
        T.wait("vector", ts[0], ts[1])
        T.op("vector", lambda e: e.scalar_tensor_tensor(
            stage[:], PB[:], fbuf2[:, 3:4], fbuf3[:], ALU.mult, ALU.add))
        T.op("vector", lambda e: e.tensor_add(stage[:], stage[:], fbuf[:]))
        T.op("vector", lambda e: e.tensor_reduce(fbuf2[:, 5:6], stage[:], AX.X, ALU.add))
        vv = T.seal("vector")
        T.wait("scalar", vv[0], vv[1])
        T.op("scalar", lambda e: e.activation(stage2[:], stage[:], AF.Square, accum_out=fbuf2[:, 6:7]))
        ss = T.seal("scalar")
        T.wait("vector", ss[0], ss[1])
        T.op("vector", lambda e: e.tensor_scalar_mul(fbuf2[:, 5:6], fbuf2[:, 5:6], 1.0 / C))
        T.op("vector", lambda e: e.tensor_scalar_mul(fbuf2[:, 6:7], fbuf2[:, 6:7], 1.0 / C))
        T.op("vector", lambda e: e.scalar_tensor_tensor(
            fbuf2[:, 7:8], fbuf2[:, 5:6], -1.0, fbuf2[:, 5:6], ALU.mult, ALU.mult))
        T.op("vector", lambda e: e.tensor_add(fbuf2[:, 6:7], fbuf2[:, 6:7], fbuf2[:, 7:8]))
        T.op("vector", lambda e: e.tensor_scalar_add(fbuf2[:, 6:7], fbuf2[:, 6:7], LNEPS))
        T.op("vector", lambda e: e.reciprocal(fbuf2[:, 6:7], fbuf2[:, 6:7]))
        vv = T.seal("vector")
        T.wait("scalar", vv[0], vv[1])
        T.op("scalar", lambda e: e.sqrt(fbuf2[:, 8:9], fbuf2[:, 6:7]))
        T.op("scalar", lambda e: e.activation(
            fbuf2[:, 9:10], fbuf2[:, 5:6], AF.Identity, scale=-1.0))
        ss = T.seal("scalar")
        T.wait("vector", ss[0], ss[1])
        T.op("vector", lambda e: e.tensor_mul(fbuf2[:, 10:11], fbuf2[:, 9:10], fbuf2[:, 8:9]))
        vv = T.seal("vector")
        T.wait("scalar", vv[0], vv[1])
        T.op("scalar", lambda e: e.activation(
            fbuf[:], stage[:], AF.Identity, bias=fbuf2[:, 10:11], scale=fbuf2[:, 8:9]))
        ss = T.seal("scalar")
        T.wait("sync", ss[0], ss[1])
        T.dma("sync", lambda e, r0=r0: e.dma_start(out=sPart[r0:r0 + P, :], in_=fbuf[:]))
        T.wait("vector", ss[0], ss[1])
        T.op("vector", lambda e: e.tensor_reduce(fbuf2[:, 11:12], fbuf[:], AX.X, ALU.max, True))
        T.op("vector", lambda e: e.tensor_scalar(fbuf2[:, 11:12], fbuf2[:, 11:12], 1.0 / 127.0, QEPS, ALU.mult, ALU.add))
        T.op("vector", lambda e: e.reciprocal(fbuf2[:, 12:13], fbuf2[:, 11:12]))
        T.op("vector", lambda e: e.tensor_scalar(stage[:], fbuf[:], fbuf2[:, 12:13], MAGIC, ALU.mult, ALU.add))
        T.op("vector", lambda e: e.tensor_scalar(tmb2[:], stage[:], -MAGIC, fbuf2[:, 11:12], ALU.add, ALU.mult))
        vv = T.seal("vector")
        T.guard("tensor", "scalar")
        T.wait("tensor", vv[0], vv[1])
        for kc in range(NKC):
            T.op("tensor", lambda e, kc=kc: e.matmul(
                PA[:, kc * P:(kc + 1) * P], tmb2[:, kc * P:(kc + 1) * P], ID,
                start=True, stop=True))
        ts = T.seal("tensor")
        T.wait("vector", ts[0], ts[1])
        T.op("vector", lambda e: e.tensor_copy(xqf_s[:], PA[:]))
        vv = T.seal("vector")
        T.wait("sync", vv[0], vv[1])
        T.dma("sync", lambda e, r0=r0: e.dma_start(
            out=sXQ.rearrange("kc p t -> p kc t")[:, :, r0:r0 + P],
            in_=xqf_s[:].rearrange("p (kc t) -> p kc t", t=P)))
        T.bar()

    sc6b.close()
    cut(9)

    sc8 = phase_scope()
    xq8_s = psb(sc8, "xq8", (P, NKC * C), BF16)
    T.dma("gpsimd", lambda e: e.dma_start(
        out=xq8_s[:].rearrange("p (kc t) -> p kc t", t=C),
        in_=sXQ.rearrange("kc p t -> p kc t")))
    hmax = psb(sc8, "hmax", (P, C), F32)
    sq8 = psb(sc8, "sq8", (P, 2 * TT), F32)
    wqg = psb(sc8, "wqg", (P, NKC * 256), BF16)
    wqu = psb(sc8, "wqu", (P, NKC * 256), BF16)
    hbl = psb(sc8, "hbl", (P, 2 * TT), BF16)
    gb = psb(sc8, "gb", (P, 2 * TT), BF16)
    ub = psb(sc8, "ub", (P, 2 * TT), BF16)
    T.op("vector", lambda e: e.memset(hmax[:], 0.0))
    T.wq("tensor", "gpsimd")

    IB2 = 256
    NIB2 = IFF // IB2
    wq_seal = {}

    def wquant4(dstg, dstu, ib2):
        subs = [(w_gT, dstg, 0, 0), (w_gT, dstg, 1, 0),
                (w_uT, dstu, 0, 1), (w_uT, dstu, 1, 1)]
        toks = {}

        def issue(k):
            wap, dst, qq, wi = subs[k]
            buf = stage if k % 2 == 0 else stage2
            T.guard("scalar", "vector")
            toks[k] = T.dma("scalar", lambda e, wap=wap, ib2=ib2, qq=qq, buf=buf: e.dma_start(
                out=buf[:].rearrange("p (ki i) -> p ki i", i=IB2),
                in_=wap.rearrange("(ki p) i -> p ki i", p=P)[:, 8 * qq:8 * qq + 8,
                                                            ib2 * IB2:(ib2 + 1) * IB2]))

        issue(0)
        issue(1)
        for k in range(4):
            wap, dst, qq, wi = subs[k]
            buf = stage if k % 2 == 0 else stage2
            tok = toks[k]
            T.wait("scalar", tok[0], tok[1])
            T.op("scalar", lambda e, wi=wi, buf=buf: e.activation(
                buf[:], buf[:], AF.Identity, scale=wsA[:, 3 + wi:4 + wi], bias=MAGIC))
            ss = T.seal("scalar")
            T.wait("vector", ss[0], ss[1])
            T.op("vector", lambda e, buf=buf: e.tensor_scalar(
                buf[:], buf[:], MAGIC - 1.0, MAGIC + 1.0, ALU.max, ALU.min))
            T.op("vector", lambda e, dst=dst, qq=qq, buf=buf: e.tensor_scalar_add(
                dst[:, qq * 2048:(qq + 1) * 2048], buf[:], -MAGIC))
            if k + 2 < 4:
                issue(k + 2)
        vv = T.seal("vector")
        return (vv, vv)

    wqg2 = [wqg, psb(sc8, "wqg2b", (P, NKC * IB2), BF16)]
    wqu2 = [wqu, psb(sc8, "wqu2b", (P, NKC * IB2), BF16)]
    wq_seal[0] = wquant4(wqg2[0], wqu2[0], 0)
    for ib2 in range(NIB2):
        par = ib2 % 2
        if ib2 + 1 < NIB2:
            if ib2 >= 1:
                T.guard("vector", "tensor")
            wq_seal[ib2 + 1] = wquant4(wqg2[1 - par], wqu2[1 - par], ib2 + 1)
        vv = wq_seal[ib2][0]
        T.wait("tensor", vv[0], vv[1])
        vv = wq_seal[ib2][1]
        T.wait("tensor", vv[0], vv[1])
        wg_, wu_ = wqg2[par], wqu2[par]
        for tt in range(NTT):
            T.wq("gpsimd", "sync")
            T.guard("gpsimd", "vector")
            T.guard("scalar", "vector")
            T.guard("tensor", "scalar")
            T.guard("tensor", "gpsimd")
            gseals8 = {}
            vseals8 = {}
            for io in range(2):
                for ki in range(NKC):
                    T.op("tensor", lambda e, io=io, ki=ki, tt=tt, wg_=wg_: e.matmul(
                        PA[:, io * TT:(io + 1) * TT],
                        wg_[:, ki * IB2 + io * 128: ki * IB2 + (io + 1) * 128],
                        xq8_s[:, ki * C + tt * TT: ki * C + (tt + 1) * TT],
                        start=(ki == 0), stop=(ki == NKC - 1)))
                for ki in range(NKC):
                    T.op("tensor", lambda e, io=io, ki=ki, tt=tt, wu_=wu_: e.matmul(
                        PB[:, io * TT:(io + 1) * TT],
                        wu_[:, ki * IB2 + io * 128: ki * IB2 + (io + 1) * 128],
                        xq8_s[:, ki * C + tt * TT: ki * C + (tt + 1) * TT],
                        start=(ki == 0), stop=(ki == NKC - 1)))
                ts = T.seal("tensor")
                go = (io % 2) * TT
                T.wait("scalar", ts[0], ts[1])
                T.op("scalar", lambda e, io=io, go=go: e.activation(gb[:, go:go + TT], PA[:, io * TT:(io + 1) * TT], AF.Silu, scale=wsA[:, 0:1]))
                T.op("scalar", lambda e, io=io, go=go: e.activation(ub[:, go:go + TT], PB[:, io * TT:(io + 1) * TT], AF.Identity, scale=wsA[:, 1:2]))
                ss = T.seal("scalar")
                T.wait("gpsimd", ss[0], ss[1])
                T.op("gpsimd", lambda e, io=io, go=go: e.tensor_mul(hbl[:, go:go + TT], gb[:, go:go + TT], ub[:, go:go + TT]))
                gg = T.seal("gpsimd")
                gseals8[io] = gg
                T.wait("scalar", gg[0], gg[1])
                T.op("scalar", lambda e, io=io, go=go: e.activation(sq8[:, go:go + TT], hbl[:, go:go + TT], AF.Square))
                ssq = T.seal("scalar")
                T.wait("vector", ssq[0], ssq[1])
                T.op("vector", lambda e, tt=tt, go=go: e.tensor_max(
                    hmax[:, tt * TT:(tt + 1) * TT], hmax[:, tt * TT:(tt + 1) * TT], sq8[:, go:go + TT]))
                vseals8[io] = T.seal("vector")
                T.wait("sync", gg[0], gg[1])
                T.dma("sync", lambda e, ib2=ib2, io=io, tt=tt, go=go: e.dma_start(
                    out=sH[ib2 * 2 + io, :, tt * TT:(tt + 1) * TT],
                    in_=hbl[:, go:go + TT]))
    T.bar()
    T.bar()
    for s in range(NCH):
        T.guard("tensor", "vector")
        T.op("tensor", lambda e, s=s: e.matmul(
            PA[:, 0:128], hmax[:, s * P:(s + 1) * P], IDF, start=True, stop=True))
        ts = T.seal("tensor")
        T.wait("vector", ts[0], ts[1])
        T.op("vector", lambda e, s=s: e.tensor_reduce(stm[:, s:s + 1], PA[:, 0:128], AX.X, ALU.max))
    vv = T.seal("vector")
    T.wait("scalar", vv[0], vv[1])
    T.op("scalar", lambda e: e.sqrt(stm[:], stm[:]))
    ss = T.seal("scalar")
    T.wait("vector", ss[0], ss[1])
    T.op("vector", lambda e: e.tensor_scalar(stm[:], stm[:], 1.0 / 127.0, QEPS, ALU.mult, ALU.add))
    T.bar()
    sc8.close()
    cut(10)

    for s in range(NCH):
        T.op("tensor", lambda e, s=s: e.matmul(
            PA[0:1, s * P:(s + 1) * P], stm[:, s:s + 1], IDF, start=True, stop=True))
    ts = T.seal("tensor")
    T.wait("vector", ts[0], ts[1])
    T.op("vector", lambda e: e.tensor_copy(stage[0:1, 0:C], PA[0:1, 0:C]))
    T.op("vector", lambda e: e.reciprocal(stage2[0:1, 0:C], stage[0:1, 0:C]))
    vv = T.seal("vector")
    T.wait("tensor", vv[0], vv[1])
    for n4 in range(4):
        T.op("tensor", lambda e, n4=n4: e.matmul(
            PA[:, n4 * TT:(n4 + 1) * TT], ONEROW_F, stage2[0:1, n4 * TT:(n4 + 1) * TT],
            start=True, stop=True))
        T.op("tensor", lambda e, n4=n4: e.matmul(
            PB[:, n4 * TT:(n4 + 1) * TT], ONEROW_F, stage[0:1, n4 * TT:(n4 + 1) * TT],
            start=True, stop=True))
    ts = T.seal("tensor")
    T.wait("vector", ts[0], ts[1])
    T.op("vector", lambda e: e.tensor_copy(fbuf[:], PA[:]))
    T.op("vector", lambda e: e.tensor_copy(fbuf2[:], PB[:]))
    T.bar()

    sc9 = phase_scope()
    NQ9 = 8
    hq9 = [psb(sc9, "hq9a", (P, 8 * C), BF16), psb(sc9, "hq9b", (P, 8 * C), BF16)]
    wd9 = [psb(sc9, "wd9a", (P, 8 * C), BF16), psb(sc9, "wd9b", (P, 8 * C), BF16)]
    qb9 = [psb(sc9, "qb9a", (P, C), F32), psb(sc9, "qb9b", (P, C), F32)]
    q9_seal = {}

    def q9_block(q, par, icl, wd_toks):
        ic = q * 8 + icl
        buf = tmb if icl % 2 == 0 else tmb2
        qb = qb9[icl % 2]
        T.guard("sync", "vector")
        htok = T.dma("sync", lambda e, ic=ic, buf=buf: e.dma_start(out=buf[:], in_=sH[ic, :, :]))
        wd_toks.append(T.dma("sync", lambda e, ic=ic, par=par, icl=icl: e.dma_start(
            out=wd9[par][:, icl * C:(icl + 1) * C], in_=sWDQ[ic * P:(ic + 1) * P, :])))
        T.wait("vector", htok[0], htok[1])
        T.op("vector", lambda e, buf=buf, qb=qb: e.tensor_mul(qb[:], buf[:], fbuf[:]))
        T.op("vector", lambda e, qb=qb: e.tensor_scalar(qb[:], qb[:], MAGIC, -MAGIC, ALU.add, ALU.add))
        T.op("vector", lambda e, icl=icl, qb=qb, par=par: e.tensor_mul(
            hq9[par][:, icl * C:(icl + 1) * C], qb[:], fbuf2[:]))

    def q9_load(q, par):
        wd_toks = []
        for icl in range(8):
            q9_block(q, par, icl, wd_toks)
        vv = T.seal("vector")
        return (vv, vv, wd_toks)

    q9_seal[0] = q9_load(0, 0)
    ev_seal9 = {}
    st9 = {}
    for q in range(NQ9):
        par = q % 2
        pf_left = list(range(8)) if q + 1 < NQ9 else []
        pf_toks = []
        if pf_left and q >= 1:
            T.guard("vector", "tensor")
            T.guard("sync", "tensor")
        for s_ in q9_seal[q][:2]:
            T.wait("tensor", s_[0], s_[1])
        for s_ in q9_seal[q][2]:
            T.wait("tensor", s_[0], s_[1])
        hq_, wd_ = hq9[par], wd9[par]
        for s in range(NCH):
            r0 = s * CH
            pp = PA if s % 2 == 0 else PB
            if (q, s - 2) in ev_seal9:
                vs2 = ev_seal9[(q, s - 2)]
                T.wait("tensor", vs2[0], vs2[1])
            elif q >= 1 and (q - 1, NCH - 2 + s) in ev_seal9:
                vs2 = ev_seal9[(q - 1, NCH - 2 + s)]
                T.wait("tensor", vs2[0], vs2[1])
            for icl in range(8):
                for co in range(4):
                    T.op("tensor", lambda e, s=s, icl=icl, co=co, pp=pp, hq_=hq_, wd_=wd_: e.matmul(
                        pp[:, co * TT:(co + 1) * TT],
                        hq_[:, icl * C + s * P: icl * C + (s + 1) * P],
                        wd_[:, icl * C + co * TT: icl * C + (co + 1) * TT],
                        start=(icl == 0), stop=(icl == 7)))
            ts = T.seal("tensor")
            T.guard("sync", "vector")
            sptok = T.dma("sync", lambda e, r0=r0: e.dma_start(out=stage2[:], in_=sPart[r0:r0 + P, :]))
            T.wait("vector", sptok[0], sptok[1])
            pk = (q, s - 1) if s > 0 else (q - 1, NCH - 1)
            if pk in st9:
                pt = st9[pk]
                T.wait("vector", pt[0], pt[1])
            T.wait("vector", ts[0], ts[1])
            T.op("vector", lambda e, pp=pp: e.scalar_tensor_tensor(
                stage[:], pp[:], wsA[:, 2:3], stage2[:], ALU.mult, ALU.add))
            if q < NQ9 - 1:
                vv = T.seal("vector")
                T.wait("sync", vv[0], vv[1])
                st9[(q, s)] = T.dma("sync", lambda e, r0=r0: e.dma_start(
                    out=sPart[r0:r0 + P, :], in_=stage[:]))
            else:
                T.op("vector", lambda e: e.tensor_reduce(stage2[:, 0:1], stage[:], AX.X, ALU.add))
                vv = T.seal("vector")
                T.wait("scalar", vv[0], vv[1])
                T.op("scalar", lambda e: e.activation(fbuf3[:], stage[:], AF.Square, accum_out=stage2[:, 1:2]))
                ss = T.seal("scalar")
                T.wait("vector", ss[0], ss[1])
                T.op("vector", lambda e: e.tensor_scalar_mul(stage2[:, 0:1], stage2[:, 0:1], 1.0 / C))
                T.op("vector", lambda e: e.tensor_scalar_mul(stage2[:, 1:2], stage2[:, 1:2], 1.0 / C))
                T.op("vector", lambda e: e.scalar_tensor_tensor(
                    stage2[:, 2:3], stage2[:, 0:1], -1.0, stage2[:, 0:1], ALU.mult, ALU.mult))
                T.op("vector", lambda e: e.tensor_add(stage2[:, 1:2], stage2[:, 1:2], stage2[:, 2:3]))
                T.op("vector", lambda e: e.tensor_scalar_add(stage2[:, 1:2], stage2[:, 1:2], LNEPS))
                T.op("vector", lambda e: e.reciprocal(stage2[:, 1:2], stage2[:, 1:2]))
                vv = T.seal("vector")
                T.wait("scalar", vv[0], vv[1])
                T.op("scalar", lambda e: e.sqrt(stage2[:, 3:4], stage2[:, 1:2]))
                ss = T.seal("scalar")
                T.wait("vector", ss[0], ss[1])
                T.op("vector", lambda e: e.tensor_mul(stage2[:, 4:5], stage2[:, 0:1], stage2[:, 3:4]))
                T.op("vector", lambda e: e.tensor_scalar_mul(stage2[:, 4:5], stage2[:, 4:5], -1.0))
                vv = T.seal("vector")
                T.wait("scalar", vv[0], vv[1])
                T.op("scalar", lambda e: e.activation(
                    stage[:], stage[:], AF.Identity, bias=stage2[:, 4:5], scale=stage2[:, 3:4]))
                ss = T.seal("scalar")
                T.wait("sync", ss[0], ss[1])
                st9[(q, s)] = T.dma("sync", lambda e, r0=r0: e.dma_start(
                    out=out[r0:r0 + P, :], in_=stage[:]))
            ev_seal9[(q, s)] = T.seal("vector")
            if pf_left and s % 2 == 1:
                q9_block(q + 1, 1 - par, pf_left.pop(0), pf_toks)
        if q + 1 < NQ9:
            vv = T.seal("vector")
            q9_seal[q + 1] = (vv, vv, pf_toks)
    sc9.close()
    T.stopped = False
    T.wq("sync", "sync")
    T.wq("gpsimd", "gpsimd")
    T.wq("scalar", "scalar")
    T.bar()

    with nc.Block() as block:
        @block.tensor
        def _(e):
            T.replay(e, "tensor")

        @block.vector
        def _(e):
            T.replay(e, "vector")

        @block.scalar
        def _(e):
            T.replay(e, "scalar")

        @block.gpsimd
        def _(e):
            T.replay(e, "gpsimd")

        @block.sync
        def _(e):
            T.replay(e, "sync")

    es.close()
    return nc


_CACHE = {}


def _consts_np():
    bf = ml_dtypes.bfloat16
    cst = np.zeros((P, NCONST), dtype=bf)
    cst[:, CO_ID:CO_ID + 128] = np.eye(P, dtype=bf)
    s_idx = np.arange(P)
    sb_ = s_idx[:, None] // SUB
    tb_ = s_idx[None, :] // SUB
    cst[:, CO_UBD:CO_UBD + 128] = ((sb_ == tb_) & (s_idx[:, None] <= s_idx[None, :])).astype(bf)
    for rho in range(4):
        cst[SUB * rho:SUB * (rho + 1), CO_IND4 + rho] = 1.0
        cst[rho, CO_EXP4 + SUB * rho: CO_EXP4 + SUB * (rho + 1)] = 1.0
    cst[0:4, CO_ONES4:CO_ONES4 + 128] = 1.0
    tri = (np.arange(4)[:, None] < np.arange(4)[None, :]).astype(bf)
    cst[0:4, CO_TRI4:CO_TRI4 + 4] = tri
    cst[:, CO_ONEC] = 1.0
    csf = np.zeros((P, NCONSTF), dtype=np.float32)
    csf[:, 1] = MAGIC
    csf[:, 2:130] = 1.0
    csf[:, 130:258] = np.eye(P, dtype=np.float32)
    return cst, csf


def kernel(**inputs):
    key = "nc"
    if key not in _CACHE:
        _CACHE[key] = build_program()
    nc = _CACHE[key]

    x_full = np.ascontiguousarray(np.asarray(inputs["L0_attn_x"], dtype=np.float32).reshape(16384, C))
    cst, csf = _consts_np()
    s_idx = np.arange(P)
    masku = (((s_idx[:, None] // SUB) == (s_idx[None, :] // SUB)) &
             (s_idx[:, None] <= s_idx[None, :])).astype(np.uint8)

    g = lambda n: np.ascontiguousarray(np.asarray(inputs[n], dtype=np.float32))
    vec = np.zeros((P, 11 * NKC), dtype=np.float32)
    vorder = ["L0_attn_maa_x", "L0_attn_maa_r", "L0_attn_maa_k", "L0_attn_maa_v",
              "L0_attn_maa_w", "L0_attn_q_b", "L0_attn_k_b", "L0_attn_v_b",
              "L0_attn_ln_g", "L0_attn_ln_b", "L0_attn_time_decay"]
    for vi, nme in enumerate(vorder):
        vec[:, vi * NKC:(vi + 1) * NKC] = g(nme).reshape(NKC, P).T

    maarows = np.stack([g("L0_attn_maa_r"), g("L0_attn_maa_k"),
                        g("L0_attn_maa_v"), g("L0_attn_maa_w")], axis=0)
    shared = {
        "vecs": vec, "consts": cst, "constf": csf, "maarows": maarows, "masku": masku,
        "kbrow": g("L0_attn_k_b").reshape(1, C), "vbrow": g("L0_attn_v_b").reshape(1, C),
        "tdrow": g("L0_attn_time_decay").reshape(1, C),
        "w_maa1": g("L0_attn_maa_w1"), "w_maa2": g("L0_attn_maa_w2"),
        "w_td1": g("L0_attn_td_w1"), "w_td2": g("L0_attn_td_w2"),
        "w_q": g("L0_attn_q_w"), "w_k": g("L0_attn_k_w"), "w_v": g("L0_attn_v_w"),
        "w_o": g("L0_attn_o_w"),
        "w_gT": np.ascontiguousarray(g("L0_ffn_w_gate").T),
        "w_uT": np.ascontiguousarray(g("L0_ffn_w_up").T),
        "w_dT": np.ascontiguousarray(g("L0_ffn_w_down").T),
    }
    in_maps = []
    for c in range(NCORES):
        m = dict(shared)
        m["x"] = np.ascontiguousarray(x_full[c * TLOC:(c + 1) * TLOC])
        if c == 0:
            m["prevrow"] = np.zeros((1, C), dtype=np.float32)
        else:
            m["prevrow"] = np.ascontiguousarray(x_full[c * TLOC - 1:c * TLOC])
        gt = np.zeros((P, 16), dtype=np.float32)
        gt[:, 0:8] = (np.arange(8)[None, :] < c).astype(np.float32)
        gt[:, 8:16] = 1.0 - gt[:, 0:8]
        m["gates"] = gt
        in_maps.append(m)

    res = run_bass_kernel_spmd(nc, in_maps, core_ids=list(range(NCORES)))
    outp = np.concatenate([r["out"] for r in res.results], axis=0)
    return outp.reshape(1, 16384, C)


# version 11
# speedup vs baseline: 1.4315x; 1.0146x over previous
import numpy as np
import ml_dtypes
import concourse.bass as bass
import concourse.mybir as mybir
from concourse.bass_utils import run_bass_kernel_spmd
from contextlib import ExitStack

F32 = mybir.dt.float32
BF16 = mybir.dt.bfloat16
AF = mybir.ActivationFunctionType
ALU = mybir.AluOpType
AX = mybir.AxisListType

P = 128
C = 2048
NKC = 16
TLOC = 2048
TT = 512
NTT = 4
CH = 128
NCH = 16
SUB = 32
DH = 64
IFF = 8192
NIC = 64
MAGIC = 12582912.0
LNEPS = 1e-5
QEPS = 1e-5
NCORES = 8

ENGS = ("tensor", "vector", "scalar", "gpsimd", "sync")

CO_ID, CO_UBD, CO_IND4, CO_EXP4, CO_ONES4, CO_TRI4, CO_ONEC, CO_ZERO = 0, 128, 256, 260, 388, 516, 520, 521
NCONST = 649
NCONSTF = 258

V_MAAX, V_MAAR, V_MAAK, V_MAAV, V_MAAW, V_QB, V_KB, V_VB, V_LNG, V_LNB, V_TD = range(11)
MAA_COL = {0: V_MAAR, 1: V_MAAK, 2: V_MAAV, 3: V_MAAW}


class Sem:
    def __init__(self, h):
        self.h = h
        self.n = 0


class KB:

    def __init__(self, nc, es):
        self.nc = nc
        self.es = es
        self.q = {e: [] for e in ENGS}
        self.last = {e: None for e in ENGS}
        self.ms = None
        self.stopped = False

    def mksem(self, name):
        return Sem(self.es.enter_context(self.nc.semaphore(name)))

    def mkqueue(self, name, n=8):
        return {"sems": [self.mksem(name + str(i)) for i in range(n)],
                "i": 0, "n": n}

    def dma(self, eng, f):
        if self.stopped:
            return None
        q = self.dq[eng]
        i = q["i"]
        q["i"] += 1
        j = i % q["n"]
        sem = q["sems"][j]
        if i >= q["n"]:
            self.wait(eng, sem, 16 * (i // q["n"]))
        self.op(eng, f, inc=(sem, 16))
        return (sem, sem.n)

    def wq(self, eng, src_eng):
        if self.stopped:
            return
        for sem in self.dq[src_eng]["sems"]:
            if sem.n > 0:
                self.wait(eng, sem, sem.n)

    def op(self, eng, f, inc=None):
        if self.stopped:
            return None
        rec = {"f": f, "inc": None}
        if eng != "tensor":
            ms = self.ms[eng]
            rec["cw"] = (ms, ms.n)
            if inc is None:
                ms.n += 1
                rec["inc"] = (ms, 1)
        if inc is not None:
            sem, k = inc
            sem.n += k
            rec["inc"] = (sem, k)
        self.q[eng].append(rec)
        self.last[eng] = rec
        return rec

    def wait(self, eng, sem, val=None):
        if self.stopped:
            return
        v = sem.n if val is None else val
        self.q[eng].append({"f": lambda e, s=sem, v=v: e.wait_ge(s.h, v), "inc": None})

    def seal(self, eng):
        if self.stopped or self.last[eng] is None:
            return (self.ms[eng], self.ms[eng].n)
        rec = self.last[eng]
        if rec["inc"] is None:
            ms = self.ms[eng]
            ms.n += 1
            rec["inc"] = (ms, 1)
        sem, _ = rec["inc"]
        return sem, sem.n

    def guard(self, consumer, producer):
        if self.stopped or self.last[producer] is None:
            return
        s, v = self.seal(producer)
        self.wait(consumer, s, v)

    def bar(self):
        if self.stopped:
            return
        marks = {}
        for e in ENGS:
            if self.last[e] is not None:
                marks[e] = self.seal(e)
        for e in ENGS:
            for e2, (s, v) in marks.items():
                if e2 != e:
                    self.wait(e, s, v)
            for e2 in self.dq:
                self.wq(e, e2)

    def replay(self, eng_obj, eng_name):
        for rec in self.q[eng_name]:
            ins = rec["f"](eng_obj)
            cw = rec.get("cw")
            if cw is not None and cw[1] > 0:
                ins._wait_ge(cw[0].h, cw[1])
            if rec["inc"] is not None:
                s, k = rec["inc"]
                ins.then_inc(s.h, k)


def build_program():
    import os
    _PL = os.environ.get("MK_PHASE_LIMIT")
    _PL = float(_PL) if _PL else None
    nc = bass.Bass("TRN2", target_bir_lowering=False, debug=False)
    es = ExitStack()
    T = KB(nc, es)

    def din(name, shape, dt):
        return nc.dram_tensor(name, list(shape), dt, kind="ExternalInput").ap()

    def dint(name, shape, dt, **kw):
        return nc.dram_tensor(name, list(shape), dt, kind="Internal", **kw).ap()

    x_in = din("x", (TLOC, C), F32)
    prevrow = din("prevrow", (1, C), F32)
    vecs = din("vecs", (P, 11 * NKC), F32)
    gates = din("gates", (P, 16), F32)
    consts = din("consts", (P, NCONST), BF16)
    constf = din("constf", (P, NCONSTF), F32)
    masku_in = din("masku", (P, 128), mybir.dt.uint8)
    maarows = din("maarows", (4, C), F32)
    kbrow = din("kbrow", (1, C), F32)
    vbrow = din("vbrow", (1, C), F32)
    tdrow = din("tdrow", (1, C), F32)
    w_maa1 = din("w_maa1", (C, 128), F32)
    w_maa2 = din("w_maa2", (4, 32, C), F32)
    w_td1 = din("w_td1", (C, DH), F32)
    w_td2 = din("w_td2", (DH, C), F32)
    w_q = din("w_q", (C, C), F32)
    w_k = din("w_k", (C, C), F32)
    w_v = din("w_v", (C, C), F32)
    w_o = din("w_o", (C, C), F32)
    w_gT = din("w_gT", (C, IFF), F32)
    w_uT = din("w_uT", (C, IFF), F32)
    w_dT = din("w_dT", (IFF, C), F32)

    out = nc.dram_tensor("out", [TLOC, C], F32, kind="ExternalOutput").ap()

    sXR = dint("sXR", (NKC, P, TLOC), BF16)
    sXK = dint("sXK", (NKC, P, TLOC), BF16)
    sXV = dint("sXV", (NKC, P, TLOC), BF16)
    sW = dint("sW", (TLOC, C), BF16)
    sQ = dint("sQ", (NKC, P, TLOC), BF16)
    sK0 = dint("sK0", (TLOC, C), BF16)
    sV = dint("sV", (TLOC, C), BF16)
    sY = dint("sY", (TLOC, C), BF16)
    sX1 = dint("sX1", (TLOC, C), F32)
    sXQ = dint("sXQ", (NKC, P, TLOC), BF16)
    sH = dint("sH", (NIC, P, TLOC), BF16)
    sPart = dint("sPart", (TLOC, C), F32)
    sWDQ = dint("sWDQ", (IFF, C), BF16)
    cc_in = dint("cc_in", (P, C + NKC), F32)
    cc_out = dint("cc_out", (NCORES, P, C + NKC), F32, addr_space="Shared")

    ccd = T.mksem("ccd")
    T.ms = {e: T.mksem("ms_" + e) for e in ENGS}
    T.dq = {"sync": T.mkqueue("qS"), "gpsimd": T.mkqueue("qG"), "scalar": T.mkqueue("qC")}

    def cut(k):
        if _PL is not None and k >= _PL:
            T.stopped = True

    def sb(name, shape, dt):
        return es.enter_context(nc.sbuf_tensor(name, list(shape), dt))

    PA = es.enter_context(nc.psum_tensor("PA", [P, C], F32))
    PB = es.enter_context(nc.psum_tensor("PB", [P, C], F32))

    cst = sb("cst", (P, NCONST), BF16)
    msku = sb("msku", (P, 128), mybir.dt.uint8)
    csf = sb("csf", (P, NCONSTF), F32)
    vec = sb("vec", (P, 11 * NKC), F32)
    gat = sb("gat", (P, 16), F32)
    qbp = sb("qbp", (P, NKC), F32)

    scS = ExitStack()
    S_run = scS.enter_context(nc.sbuf_tensor("S_run", [P, C], F32, side="right"))
    P_bf = scS.enter_context(nc.sbuf_tensor("P_bf", [P, C], BF16, side="right"))
    Tcore = sb("Tcore", (P, NKC), F32)
    wsum2 = sb("wsum2", (P, 3), F32)
    wsA = sb("wsA", (P, 6), F32)
    stm = sb("stm", (P, NKC), F32)

    ID = cst[:, CO_ID:CO_ID + 128]
    UBD = cst[:, CO_UBD:CO_UBD + 128]
    IND4 = cst[:, CO_IND4:CO_IND4 + 4]
    EXP4 = cst[0:4, CO_EXP4:CO_EXP4 + 128]
    ONES4 = cst[0:4, CO_ONES4:CO_ONES4 + 128]
    TRI4 = cst[0:4, CO_TRI4:CO_TRI4 + 4]
    ONEC = cst[:, CO_ONEC:CO_ONEC + 1]
    ZMASK = cst[:, CO_ZERO:CO_ZERO + 128]
    ONEROW_B = cst[0:1, CO_ONES4:CO_ONES4 + 128]
    ONEROW_F = csf[0:1, 2:130]
    IDF = csf[:, 130:258]
    nc.const_aps.register(F32, MAGIC, csf[:, 1:2])

    fbuf = sb("fbuf", (P, C), F32)
    fbuf2 = sb("fbuf2", (P, C), F32)
    fbuf3 = sb("fbuf3", (P, C), F32)
    tmb = sb("tmb", (P, C), BF16)
    tmb2 = sb("tmb2", (P, C), BF16)
    stage = sb("stage", (P, 2048), F32)
    stage2 = sb("stage2", (P, 2048), F32)

    def phase_scope():
        return ExitStack()

    def psb(scope, name, shape, dt):
        return scope.enter_context(nc.sbuf_tensor(name, list(shape), dt))

    T.dma("sync", lambda e: e.dma_start(out=cst[:], in_=consts))
    T.dma("sync", lambda e: e.dma_start(out=msku[:], in_=masku_in))
    T.dma("sync", lambda e: e.dma_start(out=csf[:], in_=constf))
    T.dma("sync", lambda e: e.dma_start(out=vec[:], in_=vecs))
    T.dma("sync", lambda e: e.dma_start(out=gat[:], in_=gates))

    T.dma("sync", lambda e: e.dma_start(out=fbuf[0:1, :], in_=tdrow))
    T.wq("vector", "sync")
    T.op("vector", lambda e: e.tensor_scalar(qbp[:], vec[:, V_QB * NKC:(V_QB + 1) * NKC], 0.125, None, ALU.mult))
    T.op("vector", lambda e: e.memset(S_run[:], 0.0))
    T.op("vector", lambda e: e.memset(Tcore[:], 0.0))
    T.op("vector", lambda e: e.memset(wsum2[:], 0.0))
    T.op("vector", lambda e: e.memset(stm[:], 0.0))

    vsv = T.seal("vector")
    sc0 = phase_scope()
    TD_bc = psb(sc0, "TD_bc", (P, C), F32)
    w_maa1_b = psb(sc0, "w_maa1_b", (P, NKC * 128), BF16)
    w_maa2_lo = psb(sc0, "w_maa2_lo", (64, 2 * C), BF16)
    w_maa2_hi = psb(sc0, "w_maa2_hi", (64, 2 * C), BF16)
    w_td1_b = psb(sc0, "w_td1_b", (P, NKC * DH), BF16)
    w_td2_b = psb(sc0, "w_td2_b", (DH, C), BF16)
    prevc = psb(sc0, "prevc", (P, NKC), BF16)
    mm_lo = psb(sc0, "mm_lo", (64, TT), BF16)
    mm_hi = psb(sc0, "mm_hi", (64, TT), BF16)
    tanh_b = psb(sc0, "tanh_b", (DH, TT), BF16)
    x_tmb = psb(sc0, "x_tmb", (P, C), BF16)
    big_a = psb(sc0, "p0_xfm", (P, NKC * TT), BF16)
    big_b = psb(sc0, "p0_dx", (P, NKC * TT), BF16)
    big_c = psb(sc0, "p0_mix", (P, NKC * TT), BF16)
    big_d = big_c

    T.wait("tensor", vsv[0], vsv[1])
    for j in range(4):
        T.op("tensor", lambda e, j=j: e.matmul(
            PB[:, j * TT:(j + 1) * TT], ONEROW_F, fbuf[0:1, j * TT:(j + 1) * TT],
            start=True, stop=True))
    ts = T.seal("tensor")
    T.wait("vector", ts[0], ts[1])
    T.op("vector", lambda e: e.tensor_copy(TD_bc[:], PB[:]))
    T.bar()

    cut(1)

    stats_blocks = []
    for wi, wap in enumerate((w_gT, w_uT, w_dT)):
        flat = wap.rearrange("a b -> (a b)").rearrange("(n p m) -> n p m", p=P, m=2048)
        for b in range(flat.shape[0]):
            stats_blocks.append((wi, flat[b]))
    st_i = [0]

    st_pend = [None]

    def _stats_reduce(pend):
        buf, wi, tok, col = pend
        use_v = (col % 2 == 1)
        eng = "vector" if use_v else "scalar"
        T.wait(eng, tok[0], tok[1])
        if use_v:
            T.op("vector", lambda e, buf=buf, col=col: e.tensor_reduce(
                fbuf3[:, col:col + 1], buf[:], AX.X, ALU.add, True))
            T.op("vector", lambda e, wi=wi, col=col: e.tensor_add(
                wsum2[:, wi:wi + 1], wsum2[:, wi:wi + 1], fbuf3[:, col:col + 1]))
        else:
            T.op("scalar", lambda e, buf=buf, col=col: e.activation(
                buf[:], buf[:], AF.Abs, accum_out=fbuf3[:, col:col + 1]))
            ssx = T.seal("scalar")
            T.wait("vector", ssx[0], ssx[1])
            T.op("vector", lambda e, wi=wi, col=col: e.tensor_add(
                wsum2[:, wi:wi + 1], wsum2[:, wi:wi + 1], fbuf3[:, col:col + 1]))

    def emit_stats(n, red_eng="any"):
        for _ in range(n):
            i = st_i[0]
            if i >= len(stats_blocks):
                if st_pend[0] is not None:
                    _stats_reduce(st_pend[0])
                    st_pend[0] = None
                return
            wi, blk = stats_blocks[i]
            buf = stage if (i % 2 == 0) else stage2
            T.guard("scalar", "vector")
            tok = T.dma("scalar", lambda e, buf=buf, blk=blk: e.dma_start(out=buf[:], in_=blk))
            if st_pend[0] is not None:
                _stats_reduce(st_pend[0])
            st_pend[0] = (buf, wi, tok, i % 4)
            st_i[0] += 1

    def v3(buf, tlen=TT):
        return buf[:].rearrange("p (kc t) -> p kc t", t=tlen)

    def spill_fm(dst, buf, t0, tlen=TT):
        return lambda e: e.dma_start(
            out=dst.rearrange("kc p t -> p kc t")[:, :, t0:t0 + tlen],
            in_=v3(buf, tlen))

    def load_fm(buf, src, t0, tlen=TT):
        return lambda e: e.dma_start(
            out=v3(buf, tlen),
            in_=src.rearrange("kc p t -> p kc t")[:, :, t0:t0 + tlen])


    T.dma("gpsimd", lambda e: e.dma_start(
        out=w_maa1_b[:].rearrange("p (kc m) -> p kc m", m=128),
        in_=w_maa1.rearrange("(kc p) m -> p kc m", p=P)))
    T.dma("gpsimd", lambda e: e.dma_start(out=w_maa2_lo[0:32, 0:C], in_=w_maa2[0]))
    T.dma("gpsimd", lambda e: e.dma_start(out=w_maa2_lo[32:64, C:2 * C], in_=w_maa2[1]))
    T.dma("gpsimd", lambda e: e.dma_start(out=w_maa2_hi[0:32, 0:C], in_=w_maa2[2]))
    T.dma("gpsimd", lambda e: e.dma_start(out=w_maa2_hi[32:64, C:2 * C], in_=w_maa2[3]))
    T.dma("gpsimd", lambda e: e.dma_start(
        out=w_td1_b[:].rearrange("p (kc m) -> p kc m", m=DH),
        in_=w_td1.rearrange("(kc p) m -> p kc m", p=P)))
    T.dma("gpsimd", lambda e: e.dma_start(out=w_td2_b[:], in_=w_td2))
    T.dma("gpsimd", lambda e: e.dma_start(out=tmb[0:1, :], in_=prevrow))
    T.wq("tensor", "gpsimd")
    for kc in range(NKC):
        T.op("tensor", lambda e, kc=kc: e.matmul(
            PA[:, kc:kc + 1], tmb[0:1, kc * P:(kc + 1) * P], ONEROW_B[0:1, 0:1],
            start=True, stop=True))
    ts = T.seal("tensor")
    T.wait("vector", ts[0], ts[1])
    T.op("vector", lambda e: e.tensor_copy(prevc[:], PA[:, 0:NKC]))
    T.bar()
    cut(1.1)

    for it in range(NTT):
        t0 = it * TT
        for s in range(4):
            r0 = t0 + s * P
            T.guard("gpsimd", "tensor")
            T.dma("gpsimd", lambda e, r0=r0: e.dma_start(out=x_tmb[:], in_=x_in[r0:r0 + P, :]))
            T.wq("tensor", "gpsimd")
            T.guard("tensor", "vector")
            for kc in range(NKC):
                T.op("tensor", lambda e, kc=kc: e.matmul(
                    PA[:, kc * P:(kc + 1) * P], x_tmb[:, kc * P:(kc + 1) * P], ID,
                    start=True, stop=True))
            ts = T.seal("tensor")
            T.wait("vector", ts[0], ts[1])
            T.op("vector", lambda e, s=s: e.tensor_copy(
                v3(big_a)[:, :, s * P:(s + 1) * P],
                PA[:].rearrange("p (kc t) -> p kc t", t=P)))
        cut(1.2 + it / 100)
        xA, dB = v3(big_a), v3(big_b)
        T.op("vector", lambda e: e.tensor_sub(dB[:, :, 1:TT], xA[:, :, 0:TT - 1], xA[:, :, 1:TT]))
        T.op("vector", lambda e: e.tensor_sub(
            dB[:, :, 0:1], prevc[:].broadcast_to([P, NKC, 1]), xA[:, :, 0:1]))
        T.op("vector", lambda e: e.tensor_copy(prevc[:].broadcast_to([P, NKC, 1]), xA[:, :, TT - 1:TT]))
        for kc in range(NKC):
            T.op("vector", lambda e, kc=kc: e.scalar_tensor_tensor(
                big_c[:, kc * TT:(kc + 1) * TT], big_b[:, kc * TT:(kc + 1) * TT],
                vec[:, V_MAAX * NKC + kc:V_MAAX * NKC + kc + 1],
                big_a[:, kc * TT:(kc + 1) * TT], ALU.mult, ALU.add))
        vs = T.seal("vector")
        cut(1.3 + it / 100)
        T.wait("tensor", vs[0], vs[1])
        T.guard("tensor", "scalar")
        import os as _os
        if _os.environ.get("MK_SKIP_MM") == "2":
            T.op("tensor", lambda e: e.matmul(
                PA[:, 0:TT], x_tmb[:, 0:128], x_tmb[:, 0:TT], start=True, stop=True))
        elif _os.environ.get("MK_SKIP_MM") == "3":
            T.op("tensor", lambda e: e.matmul(
                PA[:, 0:TT], w_maa1_b[:, 0:128], x_tmb[:, 0:TT], start=True, stop=True))
        elif _os.environ.get("MK_SKIP_MM") == "4":
            T.op("tensor", lambda e: e.matmul(
                PA[:, 0:TT], x_tmb[:, 0:128], big_c[:, 0:TT], start=True, stop=True))
        elif _os.environ.get("MK_SKIP_MM"):
            T.op("tensor", lambda e: e.matmul(
                PA[:, 0:TT], w_maa1_b[:, 0:128], big_c[:, 0:TT], start=True, stop=True))
        else:
            for ki in range(NKC):
                T.op("tensor", lambda e, ki=ki: e.matmul(
                    PA[:, 0:TT], w_maa1_b[:, ki * 128:(ki + 1) * 128],
                    big_c[:, ki * TT:(ki + 1) * TT], start=(ki == 0), stop=(ki == NKC - 1)))
        ts = T.seal("tensor")
        import os as _os2
        if _os2.environ.get("MK_NO_TANH"):
            T.wait("vector", ts[0], ts[1])
            T.op("vector", lambda e: e.tensor_copy(fbuf2[:, 0:TT], PA[:, 0:TT]))
            T.op("vector", lambda e: e.tensor_copy(fbuf2[:, 0:TT], fbuf2[:, 0:TT]))
        T.wait("scalar", ts[0], ts[1])
        T.op("scalar", lambda e: e.activation(fbuf2[:, 0:TT], PA[:, 0:TT], AF.Tanh)) if not _os2.environ.get("MK_NO_TANH") else None
        ssm = T.seal("scalar")
        T.wait("vector", ssm[0], ssm[1])
        T.op("vector", lambda e: e.tensor_copy(mm_lo[:], fbuf2[0:64, 0:TT]))
        T.op("vector", lambda e: e.tensor_copy(mm_hi[:], fbuf2[64:128, 0:TT]))
        ssm = T.seal("vector")
        T.wait("tensor", ssm[0], ssm[1])
        for r in (0, 1, 2, 3):
            obuf = big_d if r == 3 else big_c
            T.wq("gpsimd", "sync")
            gseals = {}
            for mo in range(NKC):
                off = (mo % 2) * TT + 512
                foff = (mo % 2) * TT
                T.guard("tensor", "vector")
                if mo >= 2:
                    gs2 = gseals[mo - 2]
                    T.wait("vector", gs2[0], gs2[1])
                elif r > 0:
                    T.guard("vector", "gpsimd")
                wsl = (w_maa2_lo if r < 2 else w_maa2_hi)[32 * (r % 2):32 * (r % 2 + 1),
                                                           (r % 2) * C + mo * 128:(r % 2) * C + (mo + 1) * 128]
                msl = (mm_lo if r < 2 else mm_hi)[32 * (r % 2):32 * (r % 2 + 1), :]
                T.op("tensor", lambda e, off=off, wsl=wsl, msl=msl: e.matmul(
                    PA[:, off:off + TT], wsl, msl, start=True, stop=True))
                ts = T.seal("tensor")
                T.wait("vector", ts[0], ts[1])
                T.op("vector", lambda e, r=r, mo=mo, off=off, foff=foff: e.scalar_tensor_tensor(
                    fbuf[:, foff:foff + TT], PA[:, off:off + TT],
                    vec[:, MAA_COL[r] * NKC + mo:MAA_COL[r] * NKC + mo + 1],
                    big_b[:, mo * TT:(mo + 1) * TT], ALU.add, ALU.mult))
                vv = T.seal("vector")
                T.wait("gpsimd", vv[0], vv[1])
                T.op("gpsimd", lambda e, mo=mo, obuf=obuf, foff=foff: e.tensor_add(
                    obuf[:, mo * TT:(mo + 1) * TT], fbuf[:, foff:foff + TT],
                    big_a[:, mo * TT:(mo + 1) * TT]))
                gseals[mo] = T.seal("gpsimd")
            gs = T.seal("gpsimd")
            if r < 3:
                T.wait("sync", gs[0], gs[1])
                dst = (sXR, sXK, sXV)[r]
                T.dma("sync", spill_fm(dst, big_c, t0))
        cut(1.5 + it / 100)
        T.guard("tensor", "gpsimd")
        T.guard("tensor", "vector")
        T.guard("tensor", "scalar")
        for ki in range(NKC):
            T.op("tensor", lambda e, ki=ki: e.matmul(
                PA[0:DH, 0:TT], w_td1_b[:, ki * DH:(ki + 1) * DH],
                big_d[:, ki * TT:(ki + 1) * TT], start=(ki == 0), stop=(ki == NKC - 1)))
        ts = T.seal("tensor")
        T.wait("scalar", ts[0], ts[1])
        T.op("scalar", lambda e: e.activation(tanh_b[:], PA[0:DH, 0:TT], AF.Tanh))
        ssm = T.seal("scalar")
        T.wait("tensor", ssm[0], ssm[1])
        for s in range(4):
            T.guard("tensor", "vector")
            for j in range(4):
                T.op("tensor", lambda e, s=s, j=j: e.matmul(
                    PB[:, j * TT:(j + 1) * TT], tanh_b[:, s * P:(s + 1) * P],
                    w_td2_b[:, j * TT:(j + 1) * TT], start=True, stop=True))
            ts = T.seal("tensor")
            T.wait("vector", ts[0], ts[1])
            T.op("vector", lambda e: e.tensor_add(fbuf2[:], PB[:], TD_bc[:]))
            vv = T.seal("vector")
            T.wait("scalar", vv[0], vv[1])
            T.op("scalar", lambda e: e.activation(fbuf2[:], fbuf2[:], AF.Exp))
            ssm = T.seal("scalar")
            T.wait("vector", ssm[0], ssm[1])
            T.wq("vector", "sync")
            T.op("vector", lambda e: e.tensor_scalar(tmb2[:], fbuf2[:], -1.0, -5.0, ALU.mult, ALU.max))
            vv = T.seal("vector")
            T.wait("sync", vv[0], vv[1])
            T.dma("sync", lambda e, r0=t0 + s * P: e.dma_start(out=sW[r0:r0 + P, :], in_=tmb2[:]))
        cut(1.6 + it / 100)
        emit_stats(8)
        T.bar()

    sc0.close()
    cut(2)

    def tm_projection(tag, wap, xsrc, biasrow_idx, is_k):
        scp = phase_scope()
        wgt = psb(scp, "wgt_" + tag, (P, NKC * C), BF16)
        big_a = [psb(scp, "xin_" + tag, (P, NKC * TT), BF16),
                 psb(scp, "xin2_" + tag, (P, NKC * TT), BF16)]
        bias_b = psb(scp, "bias_" + tag, (1, C), BF16)
        brow_in = kbrow if biasrow_idx == 0 else vbrow
        T.dma("sync", lambda e: e.dma_start(out=fbuf[0:1, :], in_=brow_in))
        T.wq("vector", "sync")
        T.op("vector", lambda e: e.tensor_copy(bias_b[:], fbuf[0:1, :]))
        T.guard("scalar", "vector")
        T.dma("gpsimd", lambda e: e.dma_start(
            out=wgt[:].rearrange("p (ki co) -> p ki co", co=C),
            in_=wap.rearrange("(ki p) co -> p ki co", p=P)))
        T.wq("tensor", "gpsimd")
        T.guard("tensor", "vector")
        atok = {}

        def a_load(it):
            T.guard("sync", "tensor")
            atok[it] = T.dma("sync", load_fm(big_a[it % 2], xsrc, it * TT))

        a_load(0)
        sptok = {}
        for it in range(NTT):
            t0 = it * TT
            if it + 1 < NTT:
                a_load(it + 1)
            tok = atok[it]
            T.wait("tensor", tok[0], tok[1])
            ba = big_a[it % 2]
            for s in range(4):
                r0 = t0 + s * P
                pp = PA if s % 2 == 0 else PB
                T.guard("tensor", "vector")
                for co in range(4):
                    for ki in range(NKC):
                        T.op("tensor", lambda e, s=s, co=co, ki=ki, pp=pp, ba=ba: e.matmul(
                            pp[:, co * TT:(co + 1) * TT],
                            ba[:, ki * TT + s * P: ki * TT + (s + 1) * P],
                            wgt[:, ki * C + co * TT: ki * C + (co + 1) * TT],
                            start=(ki == 0), stop=False))
                for co in range(4):
                    T.op("tensor", lambda e, co=co, pp=pp: e.matmul(
                        pp[:, co * TT:(co + 1) * TT], ONEROW_B,
                        bias_b[0:1, co * TT:(co + 1) * TT], start=False, stop=True))
                ts = T.seal("tensor")
                pk = (it, s - 1) if s > 0 else (it - 1, 3)
                if pk in sptok:
                    pt = sptok[pk]
                    T.wait("vector", pt[0], pt[1])
                if is_k:
                    T.guard("sync", "scalar")
                    wtok = T.dma("sync", lambda e, r0=r0: e.dma_start(out=tmb[:], in_=sW[r0:r0 + P, :]))
                    T.wait("scalar", wtok[0], wtok[1])
                    T.op("scalar", lambda e: e.activation(fbuf[:], tmb[:], AF.Exp))
                    ss = T.seal("scalar")
                    T.wait("vector", ss[0], ss[1])
                    T.op("vector", lambda e: e.tensor_scalar(fbuf[:], fbuf[:], -1.0, 1.0, ALU.mult, ALU.add))
                    T.wait("vector", ts[0], ts[1])
                    T.op("vector", lambda e, pp=pp: e.tensor_mul(tmb2[:], pp[:], fbuf[:]))
                else:
                    T.wait("vector", ts[0], ts[1])
                    T.op("vector", lambda e, pp=pp: e.tensor_copy(tmb2[:], pp[:]))
                vv = T.seal("vector")
                T.wait("sync", vv[0], vv[1])
                dst = sK0 if is_k else sV
                sptok[(it, s)] = T.dma("sync", lambda e, r0=r0, dst=dst: e.dma_start(
                    out=dst[r0:r0 + P, :], in_=tmb2[:]))
            emit_stats(8)
        T.bar()
        scp.close()

    tm_projection("k", w_k, sXK, 0, True)
    cut(3)
    tm_projection("v", w_v, sXV, 1, False)
    cut(4)

    def scan_bufs(scope, tag, scan):
        B = {}
        B["S4b"] = psb(scope, "S4b" + tag, (4, C), BF16)
        B["S4n"] = psb(scope, "S4n" + tag, (4, C), BF16)
        B["S4h"] = psb(scope, "S4h" + tag, (4, C), BF16)
        B["Bx4"] = psb(scope, "Bx4" + tag, (4, C), BF16)
        B["T_fm"] = psb(scope, "T_fm" + tag, (P, NKC), F32)
        B["eT"] = psb(scope, "eT" + tag, (P, NKC), F32)
        B["EnT"] = psb(scope, "EnT" + tag, (P, C), BF16)
        B["Ktil"] = psb(scope, "Ktil" + tag, (P, C), BF16)
        B["wrow"] = [psb(scope, "wrow0" + tag, (P, C), BF16), psb(scope, "wrow1" + tag, (P, C), BF16)]
        B["krow"] = [psb(scope, "krow0" + tag, (P, C), BF16), psb(scope, "krow1" + tag, (P, C), BF16)]
        B["vrow"] = [psb(scope, "vrow0" + tag, (P, C), BF16), psb(scope, "vrow1" + tag, (P, C), BF16)]
        if scan:
            B["Kbar"] = psb(scope, "Kbar" + tag, (P, C), BF16)
            B["k0f"] = psb(scope, "k0f" + tag, (P, C), BF16)
            B["Esub"] = psb(scope, "Esub" + tag, (P, C), BF16)
            B["Ensb"] = psb(scope, "Ensb" + tag, (P, C), BF16)
            B["Eful"] = psb(scope, "Eful" + tag, (P, C), BF16)
            B["Eq32"] = psb(scope, "Eq32" + tag, (P, C), BF16)
            B["Qtl"] = psb(scope, "Qtl" + tag, (P, C), BF16)
            B["Qht"] = psb(scope, "Qht" + tag, (P, C), BF16)
            B["Q32"] = psb(scope, "Q32" + tag, (P, C), BF16)
            B["Khat"] = psb(scope, "Khat" + tag, (P, C), BF16)
            B["Asb"] = psb(scope, "Asb" + tag, (P, 512), BF16)
            B["yb"] = psb(scope, "yb" + tag, (P, C), BF16)
            B["q512"] = psb(scope, "q512" + tag, (P, NKC * TT), BF16)
            B["D32"] = psb(scope, "D32" + tag, (P, 4 * NKC), F32)
            B["M1"] = psb(scope, "M1" + tag, (P, C), BF16)
            B["M2"] = psb(scope, "M2" + tag, (P, C), BF16)
            B["M3"] = psb(scope, "M3" + tag, (P, C), BF16)
        return B

    ch_toks = {}

    def chunk_loads(B, j, scan):
        r0 = j * CH
        par = j % 2
        toks = []
        toks.append(T.dma("sync", lambda e, r0=r0, B=B, par=par: e.dma_start(
            out=B["wrow"][par][:], in_=sW[r0:r0 + P, :])))
        toks.append(T.dma("sync", lambda e, r0=r0, B=B, par=par: e.dma_start(
            out=B["krow"][par][:], in_=sK0[r0:r0 + P, :])))
        toks.append(T.dma("sync", lambda e, r0=r0, B=B, par=par: e.dma_start(
            out=B["vrow"][par][:], in_=sV[r0:r0 + P, :])))
        ch_toks[j] = toks

    q_toks = {}

    def q512_load(B, g):
        T.guard("sync", "vector")
        q_toks[g] = T.dma("sync", load_fm(B["q512"], sQ, g * TT))

    def chunk_wait(j, scan=False):
        toks = list(ch_toks[j])
        if scan and j % 4 == 0:
            toks.append(q_toks[j // 4])
        for tok in toks:
            T.wait("tensor", tok[0], tok[1])
            T.wait("vector", tok[0], tok[1])
            T.wait("scalar", tok[0], tok[1])

    def chunk_prep(B, j, scan):
        par = j % 2
        wr, kr = B["wrow"][par], B["krow"][par]
        """32-block machinery: S32 sums, prefix, decay refs, FM/TM factors."""
        S4b, S4n, S4h, Bx4 = B["S4b"], B["S4n"], B["S4h"], B["Bx4"]
        T_fm, eT, EnT, Ktil = B["T_fm"], B["eT"], B["EnT"], B["Ktil"]
        T.guard("tensor", "vector")
        T.guard("tensor", "scalar")
        for n in range(4):
            T.op("tensor", lambda e, n=n, wr=wr, kr=kr: e.matmul(
                PA[0:4, n * TT:(n + 1) * TT], IND4, wr[:, n * TT:(n + 1) * TT],
                start=True, stop=True))
        for kc in range(NKC):
            T.op("tensor", lambda e, kc=kc, wr=wr, kr=kr: e.matmul(
                PB[:, kc:kc + 1], wr[:, kc * P:(kc + 1) * P], ONEC,
                start=True, stop=True))
        ts = T.seal("tensor")
        T.wait("vector", ts[0], ts[1])
        T.op("vector", lambda e, S4b=S4b: e.tensor_copy(S4b[:], PA[0:4, :]))
        T.op("vector", lambda e, S4b=S4b, S4n=S4n: e.tensor_scalar(S4n[:], S4b[:], -1.0, None, ALU.mult))
        T.op("vector", lambda e, S4b=S4b, S4h=S4h: e.tensor_scalar(S4h[:], S4b[:], -0.5, None, ALU.mult))
        T.op("vector", lambda e, T_fm=T_fm: e.tensor_copy(T_fm[:], PB[:, 0:NKC]))
        vv = T.seal("vector")
        T.wait("scalar", vv[0], vv[1])
        T.op("scalar", lambda e, T_fm=T_fm, eT=eT: e.activation(eT[:], T_fm[:], AF.Exp))
        T.wait("tensor", vv[0], vv[1])
        for n in range(4):
            T.op("tensor", lambda e, n=n, S4b=S4b: e.matmul(
                PA[0:4, n * TT:(n + 1) * TT], TRI4, S4b[0:4, n * TT:(n + 1) * TT],
                start=True, stop=True))
        ts = T.seal("tensor")
        T.wait("vector", ts[0], ts[1])
        T.op("vector", lambda e, Bx4=Bx4: e.tensor_copy(Bx4[:], PA[0:4, :]))
        vv = T.seal("vector")
        T.wait("tensor", vv[0], vv[1])
        T.guard("tensor", "scalar")
        for n in range(4):
            T.op("tensor", lambda e, n=n, wr=wr, kr=kr: e.matmul(
                PB[:, n * TT:(n + 1) * TT], UBD, wr[:, n * TT:(n + 1) * TT],
                start=True, stop=False))
            T.op("tensor", lambda e, n=n, Bx4=Bx4: e.matmul(
                PB[:, n * TT:(n + 1) * TT], EXP4, Bx4[0:4, n * TT:(n + 1) * TT],
                start=False, stop=False))
            T.op("tensor", lambda e, n=n, S4n=S4n: e.matmul(
                PB[:, n * TT:(n + 1) * TT], ONES4, S4n[0:4, n * TT:(n + 1) * TT],
                start=False, stop=True))
        ts = T.seal("tensor")
        T.wait("scalar", ts[0], ts[1])
        T.op("scalar", lambda e, EnT=EnT: e.activation(EnT[:], PB[:], AF.Exp, scale=-1.0))
        ss = T.seal("scalar")
        T.wait("vector", ss[0], ss[1])
        T.op("vector", lambda e, Ktil=Ktil, kr=kr: e.tensor_mul(Ktil[:], kr[:], EnT[:]))

        if scan:
            Kbar, k0f = B["Kbar"], B["k0f"]
            Esub, Ensb, Eful, Eq32 = B["Esub"], B["Ensb"], B["Eful"], B["Eq32"]
            Qtl, Qht, Q32, Khat, q512 = B["Qtl"], B["Qht"], B["Q32"], B["Khat"], B["q512"]
            D32 = B["D32"]
            T.guard("tensor", "scalar")
            for n in range(4):
                T.op("tensor", lambda e, n=n, wr=wr, kr=kr: e.matmul(
                    PB[:, n * TT:(n + 1) * TT], UBD, wr[:, n * TT:(n + 1) * TT],
                    start=True, stop=False))
                T.op("tensor", lambda e, n=n, S4n=S4n: e.matmul(
                    PB[:, n * TT:(n + 1) * TT], EXP4, S4n[0:4, n * TT:(n + 1) * TT],
                    start=False, stop=True))
            ts = T.seal("tensor")
            T.wait("scalar", ts[0], ts[1])
            T.op("scalar", lambda e: e.activation(fbuf[:], PB[:], AF.Exp, scale=-1.0))
            ss = T.seal("scalar")
            T.wait("vector", ss[0], ss[1])
            T.op("vector", lambda e, Kbar=Kbar, kr=kr: e.tensor_mul(Kbar[:], kr[:], fbuf[:]))
            T.guard("tensor", "vector")
            for kc in range(NKC):
                T.op("tensor", lambda e, kc=kc, wr=wr, kr=kr: e.matmul(
                    PA[:, kc * P:(kc + 1) * P], kr[:, kc * P:(kc + 1) * P], ID,
                    start=True, stop=True))
            ts = T.seal("tensor")
            T.wait("vector", ts[0], ts[1])
            T.op("vector", lambda e, k0f=k0f: e.tensor_copy(k0f[:], PA[:]))
            vv = T.seal("vector")
            T.wait("tensor", vv[0], vv[1])
            for kc in range(NKC):
                T.op("tensor", lambda e, kc=kc, wr=wr, kr=kr: e.matmul(
                    PA[:, kc * P:(kc + 1) * P], wr[:, kc * P:(kc + 1) * P], UBD,
                    start=True, stop=False))
                T.op("tensor", lambda e, kc=kc, S4h=S4h: e.matmul(
                    PA[:, kc * P:(kc + 1) * P], S4h[0:4, kc * P:(kc + 1) * P], EXP4,
                    start=False, stop=True))
            ts = T.seal("tensor")
            T.wait("scalar", ts[0], ts[1])
            T.op("scalar", lambda e, Esub=Esub: e.activation(Esub[:], PA[:], AF.Exp))
            T.op("scalar", lambda e, Ensb=Ensb: e.activation(Ensb[:], PA[:], AF.Exp, scale=-1.0))
            ss = T.seal("scalar")
            T.guard("tensor", "scalar")
            T.guard("tensor", "vector")
            for kc in range(NKC):
                T.op("tensor", lambda e, kc=kc, wr=wr, kr=kr: e.matmul(
                    PB[:, kc * P:(kc + 1) * P], wr[:, kc * P:(kc + 1) * P], UBD,
                    start=True, stop=True))
            ts = T.seal("tensor")
            T.wait("scalar", ts[0], ts[1])
            T.op("scalar", lambda e, Eq32=Eq32: e.activation(Eq32[:], PB[:], AF.Exp))
            ss = T.seal("scalar")
            T.wait("tensor", ss[0], ss[1])
            for kc in range(NKC):
                T.op("tensor", lambda e, kc=kc, wr=wr, kr=kr: e.matmul(
                    PB[:, kc * P:(kc + 1) * P], wr[:, kc * P:(kc + 1) * P], UBD,
                    start=True, stop=False))
                T.op("tensor", lambda e, kc=kc, Bx4=Bx4: e.matmul(
                    PB[:, kc * P:(kc + 1) * P], Bx4[0:4, kc * P:(kc + 1) * P], EXP4,
                    start=False, stop=True))
            ts = T.seal("tensor")
            T.wait("scalar", ts[0], ts[1])
            T.op("scalar", lambda e, Eful=Eful: e.activation(Eful[:], PB[:], AF.Exp))
            ss = T.seal("scalar")
            T.guard("tensor", "scalar")
            for kc in range(NKC):
                T.op("tensor", lambda e, kc=kc, S4b=S4b: e.matmul(
                    PA[:, 1024 + kc * 4: 1024 + (kc + 1) * 4],
                    S4b[0:4, kc * P:(kc + 1) * P], ID[0:4, 0:4],
                    start=True, stop=True))
            ts = T.seal("tensor")
            T.wait("scalar", ts[0], ts[1])
            T.op("scalar", lambda e, D32=D32: e.activation(D32[:], PA[:, 1024:1024 + 4 * NKC], AF.Exp))
            ss = T.seal("scalar")
            qv = q512[:].rearrange("p (kc t) -> p kc t", t=TT)
            j4 = j % 4
            T.wait("vector", ss[0], ss[1])
            for nm, Efac in (("Qtl", Esub), ("Qht", Eful), ("Q32", Eq32)):
                dst = B[nm]
                T.op("vector", lambda e, dst=dst, Efac=Efac, qv=qv, j4=j4: e.tensor_tensor(
                    dst[:].rearrange("p (kc t) -> p kc t", t=CH),
                    qv[:, :, j4 * CH:(j4 + 1) * CH],
                    Efac[:].rearrange("p (kc t) -> p kc t", t=CH), ALU.mult))
            T.op("vector", lambda e, Khat=Khat, k0f=k0f, Ensb=Ensb: e.tensor_mul(Khat[:], k0f[:], Ensb[:]))

    def state_update(B, j):
        Ktil, eT, T_fm = B["Ktil"], B["eT"], B["T_fm"]
        vrow = B["vrow"][j % 2]
        T.guard("tensor", "vector")
        for sp in range(NKC):
            T.op("tensor", lambda e, sp=sp, Ktil=Ktil, vrow=vrow: e.matmul(
                PA[:, sp * P:(sp + 1) * P], Ktil[:, sp * P:(sp + 1) * P],
                vrow[:, sp * P:(sp + 1) * P], start=True, stop=True))
        ts = T.seal("tensor")
        T.wait("vector", ts[0], ts[1])
        for sp in range(NKC):
            T.op("vector", lambda e, sp=sp, eT=eT: e.scalar_tensor_tensor(
                S_run[:, sp * P:(sp + 1) * P], S_run[:, sp * P:(sp + 1) * P],
                eT[:, sp:sp + 1], PA[:, sp * P:(sp + 1) * P], ALU.mult, ALU.add))
        T.op("vector", lambda e, T_fm=T_fm: e.tensor_add(Tcore[:], Tcore[:], T_fm[:]))

    sc4 = phase_scope()
    B4s = scan_bufs(sc4, "_s", scan=False)
    chunk_loads(B4s, 0, scan=False)
    for j in range(NCH):
        if j + 1 < NCH:
            if j >= 1:
                T.guard("sync", "tensor")
                T.guard("sync", "vector")
            chunk_loads(B4s, j + 1, scan=False)
        chunk_wait(j)
        chunk_prep(B4s, j, scan=False)
        state_update(B4s, j)
        emit_stats(7)
    vv = T.seal("vector")
    T.wait("sync", vv[0], vv[1])
    T.dma("sync", lambda e: e.dma_start(out=cc_in[:, 0:C], in_=S_run[:]))
    T.dma("sync", lambda e: e.dma_start(out=cc_in[:, C:C + NKC], in_=Tcore[:]))
    T.wq("gpsimd", "sync")
    T.op("gpsimd", lambda e: e.collective_compute(
        "AllGather", ALU.bypass, replica_groups=[list(range(NCORES))],
        ins=[cc_in], outs=[cc_out]), inc=(ccd, 1))
    T.bar()
    sc4.close()
    cut(5)

    emit_stats(len(stats_blocks) + 1, "vector")
    T.bar()

    for wi in range(3):
        T.guard("tensor", "vector")
        T.op("tensor", lambda e, wi=wi: e.matmul(
            PA[0:1, 0:128], wsum2[:, wi:wi + 1], IDF, start=True, stop=True))
        ts = T.seal("tensor")
        T.wait("vector", ts[0], ts[1])
        T.op("vector", lambda e: e.tensor_reduce(fbuf[0:1, 0:1], PA[0:1, 0:128], AX.X, ALU.add))
        T.op("vector", lambda e: e.tensor_scalar(
            fbuf[0:1, 0:1], fbuf[0:1, 0:1], 1.0 / (C * IFF), QEPS, ALU.mult, ALU.add))
        T.op("vector", lambda e: e.reciprocal(fbuf[0:1, 1:2], fbuf[0:1, 0:1]))
        vv = T.seal("vector")
        T.wait("tensor", vv[0], vv[1])
        T.op("tensor", lambda e: e.matmul(PB[:, 0:1], ONEROW_F, fbuf[0:1, 0:1], start=True, stop=True))
        T.op("tensor", lambda e: e.matmul(PB[:, 1:2], ONEROW_F, fbuf[0:1, 1:2], start=True, stop=True))
        ts = T.seal("tensor")
        T.wait("vector", ts[0], ts[1])
        T.op("vector", lambda e, wi=wi: e.tensor_copy(wsA[:, wi:wi + 1], PB[:, 0:1]))
        T.op("vector", lambda e, wi=wi: e.tensor_copy(wsA[:, 3 + wi:4 + wi], PB[:, 1:2]))
    T.bar()

    wdq_i = [0]
    wdq_ld = [0]
    wdq_ldtok = {}
    wdq_sptok = {}
    wdqb = [scS.enter_context(nc.sbuf_tensor("wdqb0", [P, C], BF16, side="right")),
            scS.enter_context(nc.sbuf_tensor("wdqb1", [P, C], BF16, side="right"))]
    wdqs = [stage, stage2]

    def wdq_load(k):
        while wdq_ld[0] < min(k, NIC):
            ic = wdq_ld[0]
            T.guard("gpsimd", "vector")
            wdq_ldtok[ic] = T.dma("gpsimd", lambda e, ic=ic: e.dma_start(
                out=wdqs[ic % 2][:], in_=w_dT[ic * P:(ic + 1) * P, :]))
            wdq_ld[0] += 1

    def emit_wdq(n):
        for _ in range(n):
            ic = wdq_i[0]
            if ic >= NIC:
                return
            wdq_load(ic + 2)
            buf = wdqs[ic % 2]
            tok = wdq_ldtok[ic]
            T.wait("scalar", tok[0], tok[1])
            T.op("scalar", lambda e, buf=buf: e.activation(
                buf[:], buf[:], AF.Identity, scale=wsA[:, 5:6], bias=MAGIC))
            ss = T.seal("scalar")
            T.wait("vector", ss[0], ss[1])
            T.op("vector", lambda e, buf=buf: e.tensor_scalar(
                buf[:], buf[:], MAGIC - 1.0, MAGIC + 1.0, ALU.max, ALU.min))
            if ic >= 2:
                st = wdq_sptok[ic - 2]
                T.wait("vector", st[0], st[1])
            T.op("vector", lambda e, ic=ic, buf=buf: e.tensor_scalar_add(
                wdqb[ic % 2][:], buf[:], -MAGIC))
            T.guard("gpsimd", "vector")
            wdq_sptok[ic] = T.dma("gpsimd", lambda e, ic=ic: e.dma_start(
                out=sWDQ[ic * P:(ic + 1) * P, :], in_=wdqb[ic % 2][:]))
            wdq_i[0] += 1

    sc1 = phase_scope()
    qw_s = psb(sc1, "qw_s", (P, NKC * C), BF16)
    qx_s = psb(sc1, "qx_s", (P, NKC * TT), BF16)
    qo_s = psb(sc1, "qo_s", (P, NKC * TT), BF16)
    T.dma("gpsimd", lambda e: e.dma_start(
        out=qw_s[:].rearrange("p (ki co) -> p ki co", co=C),
        in_=w_q.rearrange("(ki p) co -> p ki co", p=P)))
    T.wq("tensor", "gpsimd")
    for it in range(NTT):
        t0 = it * TT
        T.guard("sync", "tensor")
        T.dma("sync", load_fm(qx_s, sXR, t0))
        T.wq("tensor", "sync")
        for mo in range(NKC):
            pp, off = (PA, (mo % 4) * TT) if mo % 8 < 4 else (PB, (mo % 4) * TT)
            if mo >= 8:
                T.guard("tensor", "scalar")
            for ki in range(NKC):
                T.op("tensor", lambda e, mo=mo, ki=ki, pp=pp, off=off: e.matmul(
                    pp[:, off:off + TT], qw_s[:, ki * C + mo * 128: ki * C + (mo + 1) * 128],
                    qx_s[:, ki * TT:(ki + 1) * TT], start=(ki == 0), stop=(ki == NKC - 1)))
            ts = T.seal("tensor")
            T.wait("scalar", ts[0], ts[1])
            if mo == 0:
                T.wq("scalar", "sync")
            T.op("scalar", lambda e, mo=mo, pp=pp, off=off: e.activation(
                qo_s[:, mo * TT:(mo + 1) * TT], pp[:, off:off + TT], AF.Identity,
                bias=qbp[:, mo:mo + 1], scale=0.125))
        ss = T.seal("scalar")
        T.wait("sync", ss[0], ss[1])
        T.dma("sync", spill_fm(sQ, qo_s, t0))
    T.bar()
    sc1.close()
    cut(6)

    T.wait("sync", ccd, 1)
    T.op("vector", lambda e: e.memset(S_run[:], 0.0))
    for i in range(NCORES - 1):
        buf = stage if i % 2 == 0 else stage2
        T.guard("sync", "vector")
        T.guard("sync", "scalar")
        T.dma("sync", lambda e, i=i, buf=buf: e.dma_start(
            out=buf[:], in_=cc_out[i, :, 0:C]))
        T.dma("sync", lambda e, i=i: e.dma_start(out=fbuf[:, 0:NKC], in_=cc_out[i, :, C:C + NKC]))
        T.wq("scalar", "sync")
        T.op("scalar", lambda e: e.activation(fbuf2[:, 0:NKC], fbuf[:, 0:NKC], AF.Exp))
        ss = T.seal("scalar")
        T.wait("vector", ss[0], ss[1])
        T.op("vector", lambda e, i=i: e.tensor_scalar(
            fbuf2[:, NKC:2 * NKC], fbuf2[:, 0:NKC], gat[:, i:i + 1], gat[:, 8 + i:8 + i + 1],
            ALU.mult, ALU.add))
        T.op("vector", lambda e, i=i, buf=buf: e.tensor_scalar_mul(buf[:], buf[:], gat[:, i:i + 1]))
        for sp in range(NKC):
            T.op("vector", lambda e, sp=sp, buf=buf: e.scalar_tensor_tensor(
                S_run[:, sp * P:(sp + 1) * P], S_run[:, sp * P:(sp + 1) * P],
                fbuf2[:, NKC + sp:NKC + sp + 1], buf[:, sp * P:(sp + 1) * P],
                ALU.mult, ALU.add))
    T.op("vector", lambda e: e.tensor_copy(P_bf[:], S_run[:]))
    T.bar()
    cut(7)

    sc6 = phase_scope()
    B6 = scan_bufs(sc6, "_g", scan=True)
    chunk_loads(B6, 0, scan=True)
    q512_load(B6, 0)
    wdq_load(2)
    for j in range(NCH):
        if j + 1 < NCH:
            if j >= 1:
                T.guard("sync", "tensor")
                T.guard("sync", "vector")
            chunk_loads(B6, j + 1, scan=True)
        chunk_wait(j, scan=True)
        chunk_prep(B6, j, scan=True)
        if (j + 1) % 4 == 0 and j + 1 < NCH:
            q512_load(B6, (j + 1) // 4)
        Kbar6, Q326 = B6["Kbar"], B6["Q32"]
        vrow6 = B6["vrow"][j % 2]
        D326 = B6["D32"]
        Ms = [B6["M1"], B6["M2"], B6["M3"]]
        T.guard("tensor", "vector")
        T.guard("tensor", "scalar")
        for sig in range(3):
            if sig > 0:
                T.guard("tensor", "vector")
            for sp in range(NKC):
                tp = (sig * SUB, 0)
                T.op("tensor", lambda e, sig=sig, sp=sp, Kbar6=Kbar6, tp=tp, vrow6=vrow6: e.matmul(
                    PA[:, sp * P:(sp + 1) * P],
                    Kbar6[sig * SUB:(sig + 1) * SUB, sp * P:(sp + 1) * P],
                    vrow6[sig * SUB:(sig + 1) * SUB, sp * P:(sp + 1) * P],
                    start=True, stop=True, tile_position=tp))
            ts = T.seal("tensor")
            T.wait("vector", ts[0], ts[1])
            if sig == 0:
                T.op("vector", lambda e: e.tensor_copy(Ms[0][:], PA[:]))
            else:
                for sp in range(NKC):
                    T.op("vector", lambda e, sig=sig, sp=sp: e.scalar_tensor_tensor(
                        Ms[sig][:, sp * P:(sp + 1) * P], Ms[sig - 1][:, sp * P:(sp + 1) * P],
                        D326[:, sp * 4 + sig - 1:sp * 4 + sig], PA[:, sp * P:(sp + 1) * P],
                        ALU.mult, ALU.add))
        mv = T.seal("vector")
        Khat, Qtl, Qht, Asb, yb = B6["Khat"], B6["Qtl"], B6["Qht"], B6["Asb"], B6["yb"]
        hseal6 = {}
        tseal6 = {}
        T.wait("tensor", mv[0], mv[1])
        for h in range(32):
            sp = h // 2
            pb_ = (h % 2) * DH
            aoff = (h % 4) * 128
            ksl = Khat[pb_:pb_ + DH, sp * 128:(sp + 1) * 128]
            qsl2 = Qtl[pb_:pb_ + DH, sp * 128:(sp + 1) * 128]
            if h >= 4:
                vs4 = hseal6.get(h - 4)
                T.wait("tensor", vs4[0], vs4[1])
                ts4 = tseal6.get(h - 4)
                T.wait("vector", ts4[0], ts4[1])
            T.op("tensor", lambda e, ksl=ksl, qsl2=qsl2, aoff=aoff, pb_=pb_: e.matmul(
                PA[:, aoff:aoff + 128], ksl, qsl2, start=True, stop=True,
                tile_position=(pb_, 0)))
            ts = T.seal("tensor")
            T.wait("vector", ts[0], ts[1])
            T.op("vector", lambda e, aoff=aoff: e.memset(Asb[:, aoff:aoff + 128], 0.0))
            T.op("vector", lambda e, aoff=aoff: e.copy_predicated(
                Asb[:, aoff:aoff + 128], msku[:], PA[:, aoff:aoff + 128]))
            vv = T.seal("vector")
            hseal6[h] = vv
            T.wait("tensor", vv[0], vv[1])
            T.op("tensor", lambda e, h=h, aoff=aoff, vrow6=vrow6: e.matmul(
                PB[:, h * DH:(h + 1) * DH], Asb[:, aoff:aoff + 128],
                vrow6[:, h * DH:(h + 1) * DH], start=True, stop=False))
            for rho in range(1, 4):
                T.op("tensor", lambda e, h=h, sp=sp, pb_=pb_, rho=rho, Q326=Q326: e.matmul(
                    PB[rho * SUB:(rho + 1) * SUB, h * DH:(h + 1) * DH],
                    Q326[pb_:pb_ + DH, sp * 128 + rho * SUB: sp * 128 + (rho + 1) * SUB],
                    Ms[rho - 1][pb_:pb_ + DH, sp * 128 + pb_: sp * 128 + pb_ + DH],
                    start=False, stop=False, tile_position=(pb_, rho * SUB)))
            T.op("tensor", lambda e, h=h, sp=sp, pb_=pb_, Qht=Qht: e.matmul(
                PB[:, h * DH:(h + 1) * DH], Qht[pb_:pb_ + DH, sp * 128:(sp + 1) * 128],
                P_bf[pb_:pb_ + DH, sp * 128 + pb_: sp * 128 + pb_ + DH],
                start=False, stop=True, tile_position=(pb_, 0)))
            tseal6[h] = T.seal("tensor")
            if h % 8 == 7:
                emit_wdq(1)
        ts = T.seal("tensor")
        T.wait("vector", ts[0], ts[1])
        T.wq("vector", "sync")
        T.op("vector", lambda e, yb=yb: e.tensor_copy(yb[:], PB[:]))
        vv = T.seal("vector")
        T.wait("sync", vv[0], vv[1])
        T.dma("sync", lambda e, j=j, yb=yb: e.dma_start(out=sY[j * CH:(j + 1) * CH, :], in_=yb[:]))
        state_update(B6, j)
        T.op("vector", lambda e: e.tensor_copy(P_bf[:], S_run[:]))
    emit_wdq(NIC)
    T.bar()
    sc6.close()
    scS.close()
    cut(8)

    sc6b = phase_scope()
    ow_s = psb(sc6b, "wgt_o", (P, NKC * C), BF16)
    zg_s = psb(sc6b, "zg6b", (P, C), BF16)
    xqf_s = psb(sc6b, "xqf6b", (P, C), BF16)
    T.dma("gpsimd", lambda e: e.dma_start(
        out=ow_s[:].rearrange("p (ki co) -> p ki co", co=C),
        in_=w_o.rearrange("(ki p) co -> p ki co", p=P)))
    lnb_bf = psb(sc6b, "lnb_bf", (P, NKC), BF16)
    T.op("vector", lambda e: e.tensor_copy(lnb_bf[:], vec[:, V_LNB * NKC:(V_LNB + 1) * NKC]))
    vv = T.seal("vector")
    T.wq("tensor", "gpsimd")
    T.wait("tensor", vv[0], vv[1])
    for ki in range(NKC):
        for n4 in range(4):
            T.op("tensor", lambda e, ki=ki, n4=n4: e.matmul(
                PA[0:1, n4 * TT:(n4 + 1) * TT], lnb_bf[:, ki:ki + 1],
                ow_s[:, ki * C + n4 * TT: ki * C + (n4 + 1) * TT],
                start=(ki == 0), stop=(ki == NKC - 1)))
    ts = T.seal("tensor")
    T.wait("vector", ts[0], ts[1])
    T.op("vector", lambda e: e.tensor_copy(tmb[0:1, :], PA[0:1, 0:C]))
    vv = T.seal("vector")
    T.wait("tensor", vv[0], vv[1])
    for n in range(4):
        T.op("tensor", lambda e, n=n: e.matmul(
            PA[:, n * TT:(n + 1) * TT], ONEROW_B, tmb[0:1, n * TT:(n + 1) * TT],
            start=True, stop=True))
    ts = T.seal("tensor")
    T.wait("vector", ts[0], ts[1])
    T.op("vector", lambda e: e.tensor_copy(fbuf3[:], PA[:]))
    T.bar()

    for s in range(NCH):
        r0 = s * CH
        T.guard("sync", "vector")
        T.guard("sync", "scalar")
        T.dma("sync", lambda e, r0=r0: e.dma_start(out=tmb[:], in_=sY[r0:r0 + P, :]))
        T.dma("sync", lambda e, r0=r0: e.dma_start(out=fbuf[:], in_=x_in[r0:r0 + P, :]))
        T.wq("vector", "sync")
        T.wq("scalar", "sync")
        T.op("vector", lambda e: e.tensor_reduce(fbuf2[:, 0:1], tmb[:], AX.X, ALU.add))
        T.op("scalar", lambda e: e.activation(stage[:], tmb[:], AF.Square, accum_out=fbuf2[:, 1:2]))
        ss = T.seal("scalar")
        T.wait("vector", ss[0], ss[1])
        T.op("vector", lambda e: e.tensor_scalar_mul(fbuf2[:, 0:1], fbuf2[:, 0:1], 1.0 / C))
        T.op("vector", lambda e: e.tensor_scalar_mul(fbuf2[:, 1:2], fbuf2[:, 1:2], 1.0 / C))
        T.op("vector", lambda e: e.scalar_tensor_tensor(
            fbuf2[:, 2:3], fbuf2[:, 0:1], -1.0, fbuf2[:, 0:1], ALU.mult, ALU.mult))
        T.op("vector", lambda e: e.tensor_add(fbuf2[:, 1:2], fbuf2[:, 1:2], fbuf2[:, 2:3]))
        T.op("vector", lambda e: e.tensor_scalar_add(fbuf2[:, 1:2], fbuf2[:, 1:2], LNEPS))
        T.op("vector", lambda e: e.reciprocal(fbuf2[:, 1:2], fbuf2[:, 1:2]))
        vv = T.seal("vector")
        T.wait("scalar", vv[0], vv[1])
        T.op("scalar", lambda e: e.sqrt(fbuf2[:, 3:4], fbuf2[:, 1:2]))
        T.op("scalar", lambda e: e.activation(fbuf2[:, 4:5], fbuf2[:, 0:1], AF.Identity, scale=-1.0))
        T.op("scalar", lambda e: e.activation(tmb2[:], tmb[:], AF.Identity, bias=fbuf2[:, 4:5]))
        ss = T.seal("scalar")
        T.guard("tensor", "scalar")
        T.wait("tensor", ss[0], ss[1])
        for kc in range(NKC):
            T.op("tensor", lambda e, kc=kc: e.matmul(
                PA[:, kc * P:(kc + 1) * P], tmb2[:, kc * P:(kc + 1) * P], ID,
                start=True, stop=True))
        ts = T.seal("tensor")
        T.wait("scalar", ts[0], ts[1])
        for kc in range(NKC):
            T.op("scalar", lambda e, kc=kc: e.activation(
                zg_s[:, kc * P:(kc + 1) * P], PA[:, kc * P:(kc + 1) * P], AF.Identity,
                scale=vec[:, V_LNG * NKC + kc:V_LNG * NKC + kc + 1]))
        ss = T.seal("scalar")
        T.guard("tensor", "vector")
        T.wait("tensor", ss[0], ss[1])
        for co in range(4):
            for ki in range(NKC):
                T.op("tensor", lambda e, co=co, ki=ki: e.matmul(
                    PB[:, co * TT:(co + 1) * TT], zg_s[:, ki * P:(ki + 1) * P],
                    ow_s[:, ki * C + co * TT: ki * C + (co + 1) * TT],
                    start=(ki == 0), stop=(ki == NKC - 1)))
        ts = T.seal("tensor")
        T.wait("vector", ts[0], ts[1])
        T.op("vector", lambda e: e.scalar_tensor_tensor(
            stage[:], PB[:], fbuf2[:, 3:4], fbuf3[:], ALU.mult, ALU.add))
        T.op("vector", lambda e: e.tensor_add(stage[:], stage[:], fbuf[:]))
        T.op("vector", lambda e: e.tensor_reduce(fbuf2[:, 5:6], stage[:], AX.X, ALU.add))
        vv = T.seal("vector")
        T.wait("scalar", vv[0], vv[1])
        T.op("scalar", lambda e: e.activation(stage2[:], stage[:], AF.Square, accum_out=fbuf2[:, 6:7]))
        ss = T.seal("scalar")
        T.wait("vector", ss[0], ss[1])
        T.op("vector", lambda e: e.tensor_scalar_mul(fbuf2[:, 5:6], fbuf2[:, 5:6], 1.0 / C))
        T.op("vector", lambda e: e.tensor_scalar_mul(fbuf2[:, 6:7], fbuf2[:, 6:7], 1.0 / C))
        T.op("vector", lambda e: e.scalar_tensor_tensor(
            fbuf2[:, 7:8], fbuf2[:, 5:6], -1.0, fbuf2[:, 5:6], ALU.mult, ALU.mult))
        T.op("vector", lambda e: e.tensor_add(fbuf2[:, 6:7], fbuf2[:, 6:7], fbuf2[:, 7:8]))
        T.op("vector", lambda e: e.tensor_scalar_add(fbuf2[:, 6:7], fbuf2[:, 6:7], LNEPS))
        T.op("vector", lambda e: e.reciprocal(fbuf2[:, 6:7], fbuf2[:, 6:7]))
        vv = T.seal("vector")
        T.wait("scalar", vv[0], vv[1])
        T.op("scalar", lambda e: e.sqrt(fbuf2[:, 8:9], fbuf2[:, 6:7]))
        T.op("scalar", lambda e: e.activation(
            fbuf2[:, 9:10], fbuf2[:, 5:6], AF.Identity, scale=-1.0))
        ss = T.seal("scalar")
        T.wait("vector", ss[0], ss[1])
        T.op("vector", lambda e: e.tensor_mul(fbuf2[:, 10:11], fbuf2[:, 9:10], fbuf2[:, 8:9]))
        vv = T.seal("vector")
        T.wait("scalar", vv[0], vv[1])
        T.op("scalar", lambda e: e.activation(
            fbuf[:], stage[:], AF.Identity, bias=fbuf2[:, 10:11], scale=fbuf2[:, 8:9]))
        ss = T.seal("scalar")
        T.wait("sync", ss[0], ss[1])
        T.dma("sync", lambda e, r0=r0: e.dma_start(out=sPart[r0:r0 + P, :], in_=fbuf[:]))
        T.wait("vector", ss[0], ss[1])
        T.op("vector", lambda e: e.tensor_reduce(fbuf2[:, 11:12], fbuf[:], AX.X, ALU.max, True))
        T.op("vector", lambda e: e.tensor_scalar(fbuf2[:, 11:12], fbuf2[:, 11:12], 1.0 / 127.0, QEPS, ALU.mult, ALU.add))
        T.op("vector", lambda e: e.reciprocal(fbuf2[:, 12:13], fbuf2[:, 11:12]))
        T.op("vector", lambda e: e.tensor_scalar(stage[:], fbuf[:], fbuf2[:, 12:13], MAGIC, ALU.mult, ALU.add))
        T.op("vector", lambda e: e.tensor_scalar(tmb2[:], stage[:], -MAGIC, fbuf2[:, 11:12], ALU.add, ALU.mult))
        vv = T.seal("vector")
        T.guard("tensor", "scalar")
        T.wait("tensor", vv[0], vv[1])
        for kc in range(NKC):
            T.op("tensor", lambda e, kc=kc: e.matmul(
                PA[:, kc * P:(kc + 1) * P], tmb2[:, kc * P:(kc + 1) * P], ID,
                start=True, stop=True))
        ts = T.seal("tensor")
        T.wait("vector", ts[0], ts[1])
        T.op("vector", lambda e: e.tensor_copy(xqf_s[:], PA[:]))
        vv = T.seal("vector")
        T.wait("sync", vv[0], vv[1])
        T.dma("sync", lambda e, r0=r0: e.dma_start(
            out=sXQ.rearrange("kc p t -> p kc t")[:, :, r0:r0 + P],
            in_=xqf_s[:].rearrange("p (kc t) -> p kc t", t=P)))
        T.bar()

    sc6b.close()
    cut(9)

    sc8 = phase_scope()
    xq8_s = psb(sc8, "xq8", (P, NKC * C), BF16)
    T.dma("gpsimd", lambda e: e.dma_start(
        out=xq8_s[:].rearrange("p (kc t) -> p kc t", t=C),
        in_=sXQ.rearrange("kc p t -> p kc t")))
    hmax = psb(sc8, "hmax", (P, C), F32)
    sq8 = psb(sc8, "sq8", (P, 2 * TT), F32)
    wqg = psb(sc8, "wqg", (P, NKC * 256), BF16)
    wqu = psb(sc8, "wqu", (P, NKC * 256), BF16)
    hbl = psb(sc8, "hbl", (P, 2 * TT), BF16)
    gb = psb(sc8, "gb", (P, 2 * TT), BF16)
    ub = psb(sc8, "ub", (P, 2 * TT), BF16)
    T.op("vector", lambda e: e.memset(hmax[:], 0.0))
    T.wq("tensor", "gpsimd")

    IB2 = 256
    NIB2 = IFF // IB2
    wq_seal = {}

    def wquant4(dstg, dstu, ib2):
        subs = [(w_gT, dstg, 0, 0), (w_gT, dstg, 1, 0),
                (w_uT, dstu, 0, 1), (w_uT, dstu, 1, 1)]
        toks = {}

        def issue(k):
            wap, dst, qq, wi = subs[k]
            buf = stage if k % 2 == 0 else stage2
            T.guard("scalar", "vector")
            toks[k] = T.dma("scalar", lambda e, wap=wap, ib2=ib2, qq=qq, buf=buf: e.dma_start(
                out=buf[:].rearrange("p (ki i) -> p ki i", i=IB2),
                in_=wap.rearrange("(ki p) i -> p ki i", p=P)[:, 8 * qq:8 * qq + 8,
                                                            ib2 * IB2:(ib2 + 1) * IB2]))

        issue(0)
        issue(1)
        for k in range(4):
            wap, dst, qq, wi = subs[k]
            buf = stage if k % 2 == 0 else stage2
            tok = toks[k]
            T.wait("scalar", tok[0], tok[1])
            T.op("scalar", lambda e, wi=wi, buf=buf: e.activation(
                buf[:], buf[:], AF.Identity, scale=wsA[:, 3 + wi:4 + wi], bias=MAGIC))
            ss = T.seal("scalar")
            T.wait("vector", ss[0], ss[1])
            T.op("vector", lambda e, buf=buf: e.tensor_scalar(
                buf[:], buf[:], MAGIC - 1.0, MAGIC + 1.0, ALU.max, ALU.min))
            T.op("vector", lambda e, dst=dst, qq=qq, buf=buf: e.tensor_scalar_add(
                dst[:, qq * 2048:(qq + 1) * 2048], buf[:], -MAGIC))
            if k + 2 < 4:
                issue(k + 2)
        vv = T.seal("vector")
        return (vv, vv)

    wqg2 = [wqg, psb(sc8, "wqg2b", (P, NKC * IB2), BF16)]
    wqu2 = [wqu, psb(sc8, "wqu2b", (P, NKC * IB2), BF16)]
    wq_seal[0] = wquant4(wqg2[0], wqu2[0], 0)
    for ib2 in range(NIB2):
        par = ib2 % 2
        if ib2 + 1 < NIB2:
            if ib2 >= 1:
                T.guard("vector", "tensor")
            wq_seal[ib2 + 1] = wquant4(wqg2[1 - par], wqu2[1 - par], ib2 + 1)
        vv = wq_seal[ib2][0]
        T.wait("tensor", vv[0], vv[1])
        vv = wq_seal[ib2][1]
        T.wait("tensor", vv[0], vv[1])
        wg_, wu_ = wqg2[par], wqu2[par]
        for tt in range(NTT):
            T.wq("gpsimd", "sync")
            T.guard("gpsimd", "vector")
            T.guard("scalar", "vector")
            T.guard("tensor", "scalar")
            T.guard("tensor", "gpsimd")
            gseals8 = {}
            vseals8 = {}
            for io in range(2):
                for ki in range(NKC):
                    T.op("tensor", lambda e, io=io, ki=ki, tt=tt, wg_=wg_: e.matmul(
                        PA[:, io * TT:(io + 1) * TT],
                        wg_[:, ki * IB2 + io * 128: ki * IB2 + (io + 1) * 128],
                        xq8_s[:, ki * C + tt * TT: ki * C + (tt + 1) * TT],
                        start=(ki == 0), stop=(ki == NKC - 1)))
                for ki in range(NKC):
                    T.op("tensor", lambda e, io=io, ki=ki, tt=tt, wu_=wu_: e.matmul(
                        PB[:, io * TT:(io + 1) * TT],
                        wu_[:, ki * IB2 + io * 128: ki * IB2 + (io + 1) * 128],
                        xq8_s[:, ki * C + tt * TT: ki * C + (tt + 1) * TT],
                        start=(ki == 0), stop=(ki == NKC - 1)))
                ts = T.seal("tensor")
                go = (io % 2) * TT
                T.wait("scalar", ts[0], ts[1])
                T.op("scalar", lambda e, io=io, go=go: e.activation(gb[:, go:go + TT], PA[:, io * TT:(io + 1) * TT], AF.Silu, scale=wsA[:, 0:1]))
                T.op("scalar", lambda e, io=io, go=go: e.activation(ub[:, go:go + TT], PB[:, io * TT:(io + 1) * TT], AF.Identity, scale=wsA[:, 1:2]))
                ss = T.seal("scalar")
                T.wait("gpsimd", ss[0], ss[1])
                T.op("gpsimd", lambda e, io=io, go=go: e.tensor_mul(hbl[:, go:go + TT], gb[:, go:go + TT], ub[:, go:go + TT]))
                gg = T.seal("gpsimd")
                gseals8[io] = gg
                T.wait("scalar", gg[0], gg[1])
                T.op("scalar", lambda e, io=io, go=go: e.activation(sq8[:, go:go + TT], hbl[:, go:go + TT], AF.Square))
                ssq = T.seal("scalar")
                T.wait("vector", ssq[0], ssq[1])
                T.op("vector", lambda e, tt=tt, go=go: e.tensor_max(
                    hmax[:, tt * TT:(tt + 1) * TT], hmax[:, tt * TT:(tt + 1) * TT], sq8[:, go:go + TT]))
                vseals8[io] = T.seal("vector")
                T.wait("sync", gg[0], gg[1])
                T.dma("sync", lambda e, ib2=ib2, io=io, tt=tt, go=go: e.dma_start(
                    out=sH[ib2 * 2 + io, :, tt * TT:(tt + 1) * TT],
                    in_=hbl[:, go:go + TT]))
    T.bar()
    T.bar()
    for s in range(NCH):
        T.guard("tensor", "vector")
        T.op("tensor", lambda e, s=s: e.matmul(
            PA[:, 0:128], hmax[:, s * P:(s + 1) * P], IDF, start=True, stop=True))
        ts = T.seal("tensor")
        T.wait("vector", ts[0], ts[1])
        T.op("vector", lambda e, s=s: e.tensor_reduce(stm[:, s:s + 1], PA[:, 0:128], AX.X, ALU.max))
    vv = T.seal("vector")
    T.wait("scalar", vv[0], vv[1])
    T.op("scalar", lambda e: e.sqrt(stm[:], stm[:]))
    ss = T.seal("scalar")
    T.wait("vector", ss[0], ss[1])
    T.op("vector", lambda e: e.tensor_scalar(stm[:], stm[:], 1.0 / 127.0, QEPS, ALU.mult, ALU.add))
    T.bar()
    sc8.close()
    cut(10)

    for s in range(NCH):
        T.op("tensor", lambda e, s=s: e.matmul(
            PA[0:1, s * P:(s + 1) * P], stm[:, s:s + 1], IDF, start=True, stop=True))
    ts = T.seal("tensor")
    T.wait("vector", ts[0], ts[1])
    T.op("vector", lambda e: e.tensor_copy(stage[0:1, 0:C], PA[0:1, 0:C]))
    T.op("vector", lambda e: e.reciprocal(stage2[0:1, 0:C], stage[0:1, 0:C]))
    vv = T.seal("vector")
    T.wait("tensor", vv[0], vv[1])
    for n4 in range(4):
        T.op("tensor", lambda e, n4=n4: e.matmul(
            PA[:, n4 * TT:(n4 + 1) * TT], ONEROW_F, stage2[0:1, n4 * TT:(n4 + 1) * TT],
            start=True, stop=True))
        T.op("tensor", lambda e, n4=n4: e.matmul(
            PB[:, n4 * TT:(n4 + 1) * TT], ONEROW_F, stage[0:1, n4 * TT:(n4 + 1) * TT],
            start=True, stop=True))
    ts = T.seal("tensor")
    T.wait("vector", ts[0], ts[1])
    T.op("vector", lambda e: e.tensor_copy(fbuf[:], PA[:]))
    T.op("vector", lambda e: e.tensor_copy(fbuf2[:], PB[:]))
    T.bar()

    sc9 = phase_scope()
    NQ9 = 8
    hq9 = [psb(sc9, "hq9a", (P, 8 * C), BF16), psb(sc9, "hq9b", (P, 8 * C), BF16)]
    wd9 = [psb(sc9, "wd9a", (P, 8 * C), BF16), psb(sc9, "wd9b", (P, 8 * C), BF16)]
    qb9 = [psb(sc9, "qb9a", (P, C), F32), psb(sc9, "qb9b", (P, C), F32)]
    q9_seal = {}

    def q9_block(q, par, icl, wd_toks):
        ic = q * 8 + icl
        buf = tmb if icl % 2 == 0 else tmb2
        qb = qb9[icl % 2]
        T.guard("sync", "vector")
        htok = T.dma("sync", lambda e, ic=ic, buf=buf: e.dma_start(out=buf[:], in_=sH[ic, :, :]))
        wd_toks.append(T.dma("sync", lambda e, ic=ic, par=par, icl=icl: e.dma_start(
            out=wd9[par][:, icl * C:(icl + 1) * C], in_=sWDQ[ic * P:(ic + 1) * P, :])))
        T.wait("vector", htok[0], htok[1])
        T.op("vector", lambda e, buf=buf, qb=qb: e.tensor_mul(qb[:], buf[:], fbuf[:]))
        T.op("vector", lambda e, qb=qb: e.tensor_scalar(qb[:], qb[:], MAGIC, -MAGIC, ALU.add, ALU.add))
        T.op("vector", lambda e, icl=icl, qb=qb, par=par: e.tensor_mul(
            hq9[par][:, icl * C:(icl + 1) * C], qb[:], fbuf2[:]))

    def q9_load(q, par):
        wd_toks = []
        for icl in range(8):
            q9_block(q, par, icl, wd_toks)
        vv = T.seal("vector")
        return (vv, vv, wd_toks)

    q9_seal[0] = q9_load(0, 0)
    ev_seal9 = {}
    st9 = {}
    for q in range(NQ9):
        par = q % 2
        pf_left = list(range(8)) if q + 1 < NQ9 else []
        pf_toks = []
        if pf_left and q >= 1:
            T.guard("vector", "tensor")
            T.guard("sync", "tensor")
        for s_ in q9_seal[q][:2]:
            T.wait("tensor", s_[0], s_[1])
        for s_ in q9_seal[q][2]:
            T.wait("tensor", s_[0], s_[1])
        hq_, wd_ = hq9[par], wd9[par]
        for s in range(NCH):
            r0 = s * CH
            pp = PA if s % 2 == 0 else PB
            if (q, s - 2) in ev_seal9:
                vs2 = ev_seal9[(q, s - 2)]
                T.wait("tensor", vs2[0], vs2[1])
            elif q >= 1 and (q - 1, NCH - 2 + s) in ev_seal9:
                vs2 = ev_seal9[(q - 1, NCH - 2 + s)]
                T.wait("tensor", vs2[0], vs2[1])
            for icl in range(8):
                for co in range(4):
                    T.op("tensor", lambda e, s=s, icl=icl, co=co, pp=pp, hq_=hq_, wd_=wd_: e.matmul(
                        pp[:, co * TT:(co + 1) * TT],
                        hq_[:, icl * C + s * P: icl * C + (s + 1) * P],
                        wd_[:, icl * C + co * TT: icl * C + (co + 1) * TT],
                        start=(icl == 0), stop=(icl == 7)))
            ts = T.seal("tensor")
            T.guard("sync", "vector")
            sptok = T.dma("sync", lambda e, r0=r0: e.dma_start(out=stage2[:], in_=sPart[r0:r0 + P, :]))
            T.wait("vector", sptok[0], sptok[1])
            pk = (q, s - 1) if s > 0 else (q - 1, NCH - 1)
            if pk in st9:
                pt = st9[pk]
                T.wait("vector", pt[0], pt[1])
            T.wait("vector", ts[0], ts[1])
            T.op("vector", lambda e, pp=pp: e.scalar_tensor_tensor(
                stage[:], pp[:], wsA[:, 2:3], stage2[:], ALU.mult, ALU.add))
            if q < NQ9 - 1:
                vv = T.seal("vector")
                T.wait("sync", vv[0], vv[1])
                st9[(q, s)] = T.dma("sync", lambda e, r0=r0: e.dma_start(
                    out=sPart[r0:r0 + P, :], in_=stage[:]))
            else:
                T.op("vector", lambda e: e.tensor_reduce(stage2[:, 0:1], stage[:], AX.X, ALU.add))
                vv = T.seal("vector")
                T.wait("scalar", vv[0], vv[1])
                T.op("scalar", lambda e: e.activation(fbuf3[:], stage[:], AF.Square, accum_out=stage2[:, 1:2]))
                ss = T.seal("scalar")
                T.wait("vector", ss[0], ss[1])
                T.op("vector", lambda e: e.tensor_scalar_mul(stage2[:, 0:1], stage2[:, 0:1], 1.0 / C))
                T.op("vector", lambda e: e.tensor_scalar_mul(stage2[:, 1:2], stage2[:, 1:2], 1.0 / C))
                T.op("vector", lambda e: e.scalar_tensor_tensor(
                    stage2[:, 2:3], stage2[:, 0:1], -1.0, stage2[:, 0:1], ALU.mult, ALU.mult))
                T.op("vector", lambda e: e.tensor_add(stage2[:, 1:2], stage2[:, 1:2], stage2[:, 2:3]))
                T.op("vector", lambda e: e.tensor_scalar_add(stage2[:, 1:2], stage2[:, 1:2], LNEPS))
                T.op("vector", lambda e: e.reciprocal(stage2[:, 1:2], stage2[:, 1:2]))
                vv = T.seal("vector")
                T.wait("scalar", vv[0], vv[1])
                T.op("scalar", lambda e: e.sqrt(stage2[:, 3:4], stage2[:, 1:2]))
                ss = T.seal("scalar")
                T.wait("vector", ss[0], ss[1])
                T.op("vector", lambda e: e.tensor_mul(stage2[:, 4:5], stage2[:, 0:1], stage2[:, 3:4]))
                T.op("vector", lambda e: e.tensor_scalar_mul(stage2[:, 4:5], stage2[:, 4:5], -1.0))
                vv = T.seal("vector")
                T.wait("scalar", vv[0], vv[1])
                T.op("scalar", lambda e: e.activation(
                    stage[:], stage[:], AF.Identity, bias=stage2[:, 4:5], scale=stage2[:, 3:4]))
                ss = T.seal("scalar")
                T.wait("sync", ss[0], ss[1])
                st9[(q, s)] = T.dma("sync", lambda e, r0=r0: e.dma_start(
                    out=out[r0:r0 + P, :], in_=stage[:]))
            ev_seal9[(q, s)] = T.seal("vector")
            if pf_left and s % 2 == 1:
                q9_block(q + 1, 1 - par, pf_left.pop(0), pf_toks)
        if q + 1 < NQ9:
            vv = T.seal("vector")
            q9_seal[q + 1] = (vv, vv, pf_toks)
    sc9.close()
    T.stopped = False
    T.wq("sync", "sync")
    T.wq("gpsimd", "gpsimd")
    T.wq("scalar", "scalar")
    T.bar()

    with nc.Block() as block:
        @block.tensor
        def _(e):
            T.replay(e, "tensor")

        @block.vector
        def _(e):
            T.replay(e, "vector")

        @block.scalar
        def _(e):
            T.replay(e, "scalar")

        @block.gpsimd
        def _(e):
            T.replay(e, "gpsimd")

        @block.sync
        def _(e):
            T.replay(e, "sync")

    es.close()
    return nc


_CACHE = {}


def _consts_np():
    bf = ml_dtypes.bfloat16
    cst = np.zeros((P, NCONST), dtype=bf)
    cst[:, CO_ID:CO_ID + 128] = np.eye(P, dtype=bf)
    s_idx = np.arange(P)
    sb_ = s_idx[:, None] // SUB
    tb_ = s_idx[None, :] // SUB
    cst[:, CO_UBD:CO_UBD + 128] = ((sb_ == tb_) & (s_idx[:, None] <= s_idx[None, :])).astype(bf)
    for rho in range(4):
        cst[SUB * rho:SUB * (rho + 1), CO_IND4 + rho] = 1.0
        cst[rho, CO_EXP4 + SUB * rho: CO_EXP4 + SUB * (rho + 1)] = 1.0
    cst[0:4, CO_ONES4:CO_ONES4 + 128] = 1.0
    tri = (np.arange(4)[:, None] < np.arange(4)[None, :]).astype(bf)
    cst[0:4, CO_TRI4:CO_TRI4 + 4] = tri
    cst[:, CO_ONEC] = 1.0
    csf = np.zeros((P, NCONSTF), dtype=np.float32)
    csf[:, 1] = MAGIC
    csf[:, 2:130] = 1.0
    csf[:, 130:258] = np.eye(P, dtype=np.float32)
    return cst, csf


def kernel(**inputs):
    key = "nc"
    if key not in _CACHE:
        _CACHE[key] = build_program()
    nc = _CACHE[key]

    x_full = np.ascontiguousarray(np.asarray(inputs["L0_attn_x"], dtype=np.float32).reshape(16384, C))
    cst, csf = _consts_np()
    s_idx = np.arange(P)
    masku = (((s_idx[:, None] // SUB) == (s_idx[None, :] // SUB)) &
             (s_idx[:, None] <= s_idx[None, :])).astype(np.uint8)

    g = lambda n: np.ascontiguousarray(np.asarray(inputs[n], dtype=np.float32))
    vec = np.zeros((P, 11 * NKC), dtype=np.float32)
    vorder = ["L0_attn_maa_x", "L0_attn_maa_r", "L0_attn_maa_k", "L0_attn_maa_v",
              "L0_attn_maa_w", "L0_attn_q_b", "L0_attn_k_b", "L0_attn_v_b",
              "L0_attn_ln_g", "L0_attn_ln_b", "L0_attn_time_decay"]
    for vi, nme in enumerate(vorder):
        vec[:, vi * NKC:(vi + 1) * NKC] = g(nme).reshape(NKC, P).T

    maarows = np.stack([g("L0_attn_maa_r"), g("L0_attn_maa_k"),
                        g("L0_attn_maa_v"), g("L0_attn_maa_w")], axis=0)
    shared = {
        "vecs": vec, "consts": cst, "constf": csf, "maarows": maarows, "masku": masku,
        "kbrow": g("L0_attn_k_b").reshape(1, C), "vbrow": g("L0_attn_v_b").reshape(1, C),
        "tdrow": g("L0_attn_time_decay").reshape(1, C),
        "w_maa1": g("L0_attn_maa_w1"), "w_maa2": g("L0_attn_maa_w2"),
        "w_td1": g("L0_attn_td_w1"), "w_td2": g("L0_attn_td_w2"),
        "w_q": g("L0_attn_q_w"), "w_k": g("L0_attn_k_w"), "w_v": g("L0_attn_v_w"),
        "w_o": g("L0_attn_o_w"),
        "w_gT": np.ascontiguousarray(g("L0_ffn_w_gate").T),
        "w_uT": np.ascontiguousarray(g("L0_ffn_w_up").T),
        "w_dT": np.ascontiguousarray(g("L0_ffn_w_down").T),
    }
    in_maps = []
    for c in range(NCORES):
        m = dict(shared)
        m["x"] = np.ascontiguousarray(x_full[c * TLOC:(c + 1) * TLOC])
        if c == 0:
            m["prevrow"] = np.zeros((1, C), dtype=np.float32)
        else:
            m["prevrow"] = np.ascontiguousarray(x_full[c * TLOC - 1:c * TLOC])
        gt = np.zeros((P, 16), dtype=np.float32)
        gt[:, 0:8] = (np.arange(8)[None, :] < c).astype(np.float32)
        gt[:, 8:16] = 1.0 - gt[:, 0:8]
        m["gates"] = gt
        in_maps.append(m)

    res = run_bass_kernel_spmd(nc, in_maps, core_ids=list(range(NCORES)))
    outp = np.concatenate([r["out"] for r in res.results], axis=0)
    return outp.reshape(1, 16384, C)
